# version 03
# baseline (speedup 1.0000x reference)
import numpy as np
import ml_dtypes
import concourse.bass as bass
import concourse.mybir as mybir
from concourse.bass import IndirectOffsetOnAxis
from concourse.bass_utils import run_bass_kernel_spmd

F32 = mybir.dt.float32
BF16 = mybir.dt.bfloat16
I32 = mybir.dt.int32
I8 = mybir.dt.int8
AF = mybir.ActivationFunctionType
ALU = mybir.AluOpType

NCORES = 8
T = 2048
D = 4096
G = 4
TG = T // G
NRT = TG // 128
NH = 32
HD = 128
NKV = 8
NBLK, BS, NSEL = 512, 16, 256
SCALE = float(np.float32(1.0) / np.float32(np.sqrt(np.float32(128.0))))
CSHIFT = 10.0
EPS = 1e-5
SEQ = 4096


def _host_tables(core):
    inv_freq = (np.float32(1.0) /
                np.power(np.float32(10000.0),
                         np.arange(0, HD, 2, dtype=np.float32) / np.float32(HD)))
    t = np.arange(SEQ, dtype=np.float32)
    freqs = np.outer(t, inv_freq).astype(np.float32)
    emb = np.concatenate([freqs, freqs], -1)
    cos = np.cos(emb).astype(np.float32)
    sin = np.sin(emb).astype(np.float32)
    off = (core % 2) * T
    return (np.ascontiguousarray(cos[off:off + T].T),
            np.ascontiguousarray(sin[off:off + T].T))


def _prot():
    p = np.zeros((128, 128), dtype=np.float32)
    for s in range(64):
        p[s, s + 64] = 1.0
    for s in range(64, 128):
        p[s, s - 64] = -1.0
    return p.astype(ml_dtypes.bfloat16)


class _Ctr:
    __slots__ = ("h", "v", "name")

    def __init__(self, name):
        self.h = None
        self.v = 0
        self.name = name


_SCHED_META = None
_NC_CACHE = None


def build_nc():
    global _SCHED_META
    nc = bass.Bass("TRN2", target_bir_lowering=False)

    x_in = nc.dram_tensor("x_in", [T, D], F32, kind="ExternalInput").ap()
    wq_in = nc.dram_tensor("wq_in", [D, D], F32, kind="ExternalInput").ap()
    wo_in = nc.dram_tensor("wo_in", [D, D], F32, kind="ExternalInput").ap()
    kc_in = nc.dram_tensor("kc_in", [NBLK, NKV, BS, HD], F32, kind="ExternalInput").ap()
    vc_in = nc.dram_tensor("vc_in", [NBLK, NKV, BS, HD], F32, kind="ExternalInput").ap()
    bt_in = nc.dram_tensor("bt_in", [NSEL], I32, kind="ExternalInput").ap()
    fk_in = nc.dram_tensor("fk_in", [512, D], I8, kind="ExternalInput").ap()
    cos_in = nc.dram_tensor("cos_in", [HD, T], F32, kind="ExternalInput").ap()
    sin_in = nc.dram_tensor("sin_in", [HD, T], F32, kind="ExternalInput").ap()
    y_out = nc.dram_tensor("y_out", [T, D], F32, kind="ExternalOutput").ap()

    x16 = nc.dram_tensor("x16_s", [T, D], BF16, kind="Internal").ap()
    wq16 = nc.dram_tensor("wq16_s", [D, D], BF16, kind="Internal").ap()
    wo16 = nc.dram_tensor("wo16_s", [D, D], BF16, kind="Internal").ap()
    wqT = nc.dram_tensor("wqT_s", [D, D], BF16, kind="Internal").ap()
    woT = nc.dram_tensor("woT_s", [D, D], BF16, kind="Internal").ap()
    kg = nc.dram_tensor("kg_s", [NKV, NSEL, BS, HD], BF16, kind="Internal").ap()
    vg = nc.dram_tensor("vg_s", [NKV, NSEL, BS, HD], BF16, kind="Internal").ap()
    x1f = nc.dram_tensor("x1f_s", [T, D], F32, kind="Internal").ap()
    x1b = nc.dram_tensor("x1b_s", [T, D], BF16, kind="Internal").ap()

    ones_col = nc.inline_tensor(np.ones((128, 1), dtype=np.float32), name="ones_col").ap()
    ones_row = nc.inline_tensor(np.ones((1, 128), dtype=np.float32), name="ones_row").ap()
    prot_c = nc.inline_tensor(_prot(), name="prot_c").ap()
    konst_c = nc.inline_tensor(
        np.tile(np.array([[-CSHIFT, EPS, 0.0, 0.0]], dtype=np.float32), (128, 1)),
        name="konst_c").ap()

    sbt = nc.alloc_sbuf_tensor
    qTar = sbt("qTar", [128, NH * TG], BF16)
    xcar = sbt("xcar", [128, 32 * TG], BF16)
    kvw = sbt("kvw", [128, 16384], BF16)
    slab = sbt("slab", [128, 16384], BF16)
    ebuf = sbt("ebuf", [128, 3 * 1024], BF16)
    acc = sbt("acc", [128, 2 * 1024], F32)
    attnb = sbt("attnb", [128, NRT * D], BF16)
    z = sbt("z", [128, D], F32)
    qraw = sbt("qraw", [128, 2 * TG], BF16)
    cosg = sbt("cosg", [128, TG], F32)
    sing = sbt("sing", [128, TG], F32)
    idx = sbt("idx", [128, 2], I32)
    onec = sbt("onec", [128, 1], F32)
    oner = sbt("oner", [1, 128], F32)
    prot = sbt("prot", [128, 128], BF16)
    konst = sbt("konst", [128, 4], F32)
    recip = sbt("recip", [1, 1024], F32)
    mean = sbt("mean", [128, 1], F32)
    var2 = sbt("var2", [128, 1], F32)
    varv = sbt("varv", [128, 1], F32)
    stdv = sbt("stdv", [128, 1], F32)
    rstd = sbt("rstd", [128, 1], F32)

    PA = nc.alloc_psum_tensor("PA", [128, 4096], F32)

    prog = {k: [] for k in ("sync", "gpsimd", "tensor", "vector", "scalar")}
    waited = {k: {} for k in prog}
    _SCHED_META = {k: [] for k in prog}

    names = ("sQ", "sK", "sGSt", "sXT", "sCS", "sJS0", "sJS1", "sKV0", "sKV1",
             "sCOL0", "sCOL1", "sZX", "sGX", "sGG", "sGW", "sGZ", "sGF",
             "sWB0", "sWB1", "sWB2", "sWB3",
             "sPE", "sDVE", "sACT")
    C = {n: _Ctr(n) for n in names}
    sQ, sK, sGSt, sXT, sCS = C["sQ"], C["sK"], C["sGSt"], C["sXT"], C["sCS"]
    sJS = [C["sJS0"], C["sJS1"]]
    sKV = [C["sKV0"], C["sKV1"]]
    sCOL = [C["sCOL0"], C["sCOL1"]]
    sZX, sGX, sGG, sGW, sGZ, sGF = (C["sZX"], C["sGX"], C["sGG"], C["sGW"],
                                    C["sGZ"], C["sGF"])
    sPE, sDVE, sACT = C["sPE"], C["sDVE"], C["sACT"]
    sWB = [C["sWB0"], C["sWB1"], C["sWB2"], C["sWB3"]]
    dma_ctrs = [sQ, sK, sGSt, sXT, sCS, sJS[0], sJS[1], sKV[0], sKV[1],
                sCOL[0], sCOL[1], sZX, sGX, sGG, sGW, sGZ, sGF] + sWB

    def E(eng, fn):
        prog[eng].append(fn)

    def W(eng, ctr, val):
        if val <= 0:
            return
        if val <= waited[eng].get(ctr.name, 0):
            return
        waited[eng][ctr.name] = val
        _SCHED_META[eng].append(("w", ctr.name, val))
        E(eng, lambda e, c=ctr, v=val: e.wait_ge(c.h, v))

    def DMA(qname, ctr, out, in_, **kw):
        ctr.v += 16
        _SCHED_META[qname].append(("i", ctr.name, 16))
        E(qname, lambda e, o=out, i=in_, c=ctr, kw2=dict(kw):
          e.dma_start(out=o, in_=i, **kw2).then_inc(c.h, 16))
        return ctr.v

    def OP(eng, ctr, meth, *args, **kw):
        ctr.v += 1
        _SCHED_META[eng].append(("i", ctr.name, 1))
        E(eng, lambda e, m=meth, a=args, c=ctr, kw2=dict(kw):
          getattr(e, m)(*a, **kw2).then_inc(c.h, 1))
        return ctr.v

    def MM(out, lhsT, rhs, start, stop):
        sPE.v += 1
        _SCHED_META["tensor"].append(("i", "sPE", 1))
        E("tensor", lambda e, o=out, l=lhsT, r=rhs, st=start, sp=stop, c=sPE:
          e.matmul(o, l, r, start=st, stop=sp).then_inc(c.h, 1))
        return sPE.v

    def compute_barrier():
        for eng in ("tensor", "vector", "scalar"):
            for c in (sPE, sDVE, sACT):
                W(eng, c, c.v)

    DMA("sync", sK, onec[:, :], ones_col)
    DMA("sync", sK, oner[:, :], ones_row)
    DMA("sync", sK, prot[:, :], prot_c)
    DMA("sync", sK, konst[:, :], konst_c)
    k_all = DMA("sync", sK, idx[:, :], bt_in.rearrange("(b p) -> p b", p=128),
                allow_slow_non_contiguous=True)

    for i in range(G):
        DMA("gpsimd", sGX, x16[i * TG:(i + 1) * TG, :], x_in[i * TG:(i + 1) * TG, :])
    x_cast_all = sGX.v

    kc_flat = kc_in.rearrange("b h r d -> b (h r d)")
    vc_flat = vc_in.rearrange("b h r d -> b (h r d)")
    slab_g3 = slab[:, :].rearrange("p (h rd) -> p h rd", rd=BS * HD)
    gath_done = 0
    for (src, dst) in ((kc_flat, kg), (vc_flat, vg)):
        for b in range(2):
            W("gpsimd", sK, k_all)
            W("gpsimd", sGSt, gath_done)
            sGG.v += 16
            _SCHED_META["gpsimd"].append(("i", "sGG", 16))
            E("gpsimd", lambda e, s=src, bb=b: e.indirect_dma_start(
                slab[:, :], None, s,
                IndirectOffsetOnAxis(idx[:, bb:bb + 1], 0)).then_inc(sGG.h, 16))
            W("sync", sGG, sGG.v)
            dst4 = dst[:, b * 128:(b + 1) * 128, :, :].rearrange("h i r d -> i h (r d)")
            gath_done = DMA("sync", sGSt, dst4, slab_g3)
    gath_all = gath_done

    for i in range(4):
        DMA("gpsimd", sGW, wq16[i * 1024:(i + 1) * 1024, :],
            wq_in[i * 1024:(i + 1) * 1024, :])
    wq_cast_all = sGW.v
    W("gpsimd", sGW, wq_cast_all)
    for i in range(4):
        DMA("gpsimd", sGW, wo16[i * 1024:(i + 1) * 1024, :],
            wo_in[i * 1024:(i + 1) * 1024, :])
    wo_cast_all = sGW.v

    wb_t = [0, 0, 0, 0]
    wb_s = [0, 0, 0, 0]

    def build_wT(w16_ap, wT_ap, cast_val):
        stiles = [slab[:, s * 4096:(s + 1) * 4096] for s in range(4)]

        def emit_store(dcx):
            s = dcx % 4
            W("sync", sWB[s], wb_t[s])
            wb_s[s] = DMA("sync", sQ, wT_ap[dcx * 128:(dcx + 1) * 128, :], stiles[s])

        W("sync", sGW, cast_val)
        for dc in range(32):
            s = dc % 4
            if wb_s[s]:
                W("sync", sQ, sQ.v)
            wb_t[s] = DMA("sync", sWB[s], stiles[s],
                          w16_ap[:, dc * 128:(dc + 1) * 128], transpose=True)
            if dc >= 1:
                emit_store(dc - 1)
        emit_store(31)
        return max(wb_s)

    W("sync", sGSt, gath_all)
    wT_built = build_wT(wq16, wqT, wq_cast_all)

    ebuf_free_pe = [0, 0, 0]
    ebuf_free_dve = [0, 0, 0]
    qpsum_free = [0, 0]
    rpsum_free = [0, 0]
    qraw_free_pe = [0, 0]
    qraw_free_dve = [0, 0]
    spsum_free = [0, 0]
    kv_free = [0, 0]
    slab_free = [0, 0]
    qTar_ready = {}
    ctx_ready = {}
    p3_psum_free = {}
    ctx_evac_done = 0
    tail_free_act = 0
    tail_free_dve = 0
    acc_guard = [0, 0]
    pe_sums_prev = 0
    last_zq = 0
    last_zg = 0

    def phase_p1(g):
        W("sync", sPE, sPE.v)
        W("sync", sDVE, sDVE.v)
        W("vector", sGZ, last_zg)
        W("vector", sQ, last_zq)
        DMA("sync", sCS, cosg[:, :], cos_in[:, g * TG:(g + 1) * TG])
        cs_all = DMA("sync", sCS, sing[:, :], sin_in[:, g * TG:(g + 1) * TG])
        W("sync", sGX, x_cast_all)
        xt_all = 0
        for dc in range(32):
            xt_all = DMA("sync", sXT, xcar[:, dc * TG:(dc + 1) * TG],
                         x16[g * TG:(g + 1) * TG, dc * 128:(dc + 1) * 128],
                         transpose=True)
        for jblk in range(NH):
            par = jblk % 2
            W("sync", sQ, wT_built)
            W("sync", sPE, slab_free[par])
            js = slab[:, par * 4096:(par + 1) * 4096]
            c_js = DMA("sync", sJS[par], js.rearrange("p (dc j) -> p dc j", j=128),
                       wqT[:, jblk * 128:(jblk + 1) * 128]
                       .rearrange("(dc p) j -> p dc j", p=128))
            qp = PA[:, par * TG:(par + 1) * TG]
            W("tensor", sJS[par], c_js)
            W("tensor", sXT, xt_all)
            W("tensor", sACT, qpsum_free[par])
            for dc in range(32):
                MM(qp, js[:, dc * 128:(dc + 1) * 128], xcar[:, dc * TG:(dc + 1) * TG],
                   start=(dc == 0), stop=(dc == 31))
            slab_free[par] = sPE.v
            pe_q = sPE.v
            W("scalar", sPE, pe_q)
            W("scalar", sPE, qraw_free_pe[par])
            W("scalar", sDVE, qraw_free_dve[par])
            W("scalar", sACT, qpsum_free[par])
            qpsum_free[par] = OP("scalar", sACT, "copy",
                                 qraw[:, par * TG:(par + 1) * TG], qp)
            rp = PA[:, 2048 + par * TG: 2048 + (par + 1) * TG]
            W("tensor", sACT, qpsum_free[par])
            W("tensor", sDVE, rpsum_free[par])
            pe_rot = MM(rp, prot[:, :], qraw[:, par * TG:(par + 1) * TG],
                        start=True, stop=True)
            qraw_free_pe[par] = pe_rot
            W("vector", sCS, cs_all)
            W("vector", sPE, pe_rot)
            W("vector", sACT, qpsum_free[par])
            t1 = z[:, 1024 + par * TG:1024 + (par + 1) * TG]
            OP("vector", sDVE, "tensor_mul", t1, rp, sing[:, :])
            rpsum_free[par] = sDVE.v
            OP("vector", sDVE, "tensor_mul", z[:, par * TG:(par + 1) * TG],
               qraw[:, par * TG:(par + 1) * TG], cosg[:, :])
            qraw_free_dve[par] = sDVE.v
            W("vector", sDVE, sDVE.v)
            qTar_ready[jblk] = OP("vector", sDVE, "tensor_add",
                                  qTar[:, jblk * TG:(jblk + 1) * TG],
                                  z[:, par * TG:(par + 1) * TG], t1)

    def phase_p2(g):
        nonlocal ctx_evac_done, tail_free_act, tail_free_dve, pe_sums_prev
        W("scalar", sGZ, last_zg)
        W("scalar", sQ, last_zq)
        for hk in range(NKV):
            par = hk % 2
            W("sync", sGSt, gath_all)
            W("sync", sPE, kv_free[par])
            base = par * 8192
            DMA("sync", sKV[par], kvw[:, base:base + 4096],
                kg[hk, :, :, :].rearrange("i r d -> (i r) d"), transpose=True)
            kv_ready = DMA(
                "sync", sKV[par],
                kvw[:, base + 4096:base + 8192].rearrange("p (c d) -> p c d", d=128),
                vg[hk, :, :, :].rearrange("(c a) r d -> (a r) c d", a=8))
            for pr in range(2):
                h0 = hk * 4 + 2 * pr
                h1 = h0 + 1
                W("tensor", sKV[par], kv_ready)
                W("tensor", sDVE, max(qTar_ready.get(h0, 0), qTar_ready.get(h1, 0)))
                W("vector", sPE, pe_sums_prev)
                acc_last = [0, 0]
                for j in range(32):
                    sp = j % 2
                    eslot = j % 3
                    scp = PA[:, sp * 1024:(sp + 1) * 1024]
                    kchunk = kvw[:, base + j * 128:base + (j + 1) * 128]
                    W("tensor", sACT, spsum_free[sp])
                    MM(scp[:, 0:TG], kchunk, qTar[:, h0 * TG:(h0 + 1) * TG],
                       start=True, stop=True)
                    pe_sc = MM(scp[:, TG:2 * TG], kchunk,
                               qTar[:, h1 * TG:(h1 + 1) * TG], start=True, stop=True)
                    eb = ebuf[:, eslot * 1024:(eslot + 1) * 1024]
                    W("scalar", sPE, max(pe_sc, ebuf_free_pe[eslot]))
                    W("scalar", sDVE, ebuf_free_dve[eslot])
                    W("scalar", sK, k_all)
                    act_e = OP("scalar", sACT, "activation", eb, scp, AF.Exp,
                               konst[:, 0:1], SCALE)
                    spsum_free[sp] = act_e
                    W("vector", sACT, act_e)
                    W("vector", sDVE, acc_guard[sp])
                    if j < 2:
                        acc_last[j] = OP("vector", sDVE, "tensor_copy",
                                         acc[:, sp * 1024:(sp + 1) * 1024], eb)
                    else:
                        acc_last[sp] = OP("vector", sDVE, "tensor_add",
                                          acc[:, sp * 1024:(sp + 1) * 1024],
                                          acc[:, sp * 1024:(sp + 1) * 1024], eb)
                    acc_guard[sp] = acc_last[sp]
                    ebuf_free_dve[eslot] = acc_last[sp]
                    vchunk = kvw[:, base + 4096 + j * 128:base + 4096 + (j + 1) * 128]
                    W("tensor", sACT, act_e)
                    if j == 0:
                        W("tensor", sDVE, ctx_evac_done)
                    MM(PA[:, 2048:2048 + TG], vchunk, eb[:, 0:TG],
                       start=(j == 0), stop=(j == 31))
                    pe_ctx = MM(PA[:, 2048 + TG:2048 + 2 * TG], vchunk,
                                eb[:, TG:2 * TG], start=(j == 0), stop=(j == 31))
                    ebuf_free_pe[eslot] = pe_ctx
                pe_ctx_last = sPE.v
                W("tensor", sDVE, max(acc_last))
                W("tensor", sACT, tail_free_act)
                W("tensor", sDVE, tail_free_dve)
                for half in range(2):
                    tail = PA[0:1, 3072 + half * 512:3072 + (half + 1) * 512]
                    MM(tail, onec[:, :], acc[:, half * 512:(half + 1) * 512],
                       start=True, stop=False)
                    MM(tail, onec[:, :],
                       acc[:, 1024 + half * 512:1024 + (half + 1) * 512],
                       start=False, stop=True)
                pe_sums = sPE.v
                pe_sums_prev = pe_sums
                W("vector", sPE, pe_sums)
                tail_free_dve = OP("vector", sDVE, "reciprocal", recip[:, :],
                                   PA[0:1, 3072:4096])
                W("tensor", sDVE, tail_free_dve)
                for half in range(2):
                    MM(PA[:, 3072 + half * 512:3072 + (half + 1) * 512], oner[:, :],
                       recip[0:1, half * 512:(half + 1) * 512], start=True, stop=True)
                pe_bc = sPE.v
                W("scalar", sPE, pe_bc)
                tail_free_act = OP("scalar", sACT, "copy", z[:, 0:1024],
                                   PA[:, 3072:4096])
                W("vector", sPE, pe_ctx_last)
                W("vector", sACT, tail_free_act)
                OP("vector", sDVE, "tensor_mul", xcar[:, h0 * TG:(h0 + 1) * TG],
                   PA[:, 2048:2048 + TG], z[:, 0:TG])
                ctx_ready[h0] = sDVE.v
                OP("vector", sDVE, "tensor_mul", xcar[:, h1 * TG:(h1 + 1) * TG],
                   PA[:, 2048 + TG:2048 + 2 * TG], z[:, TG:2 * TG])
                ctx_ready[h1] = sDVE.v
                ctx_evac_done = sDVE.v
            kv_free[par] = sPE.v

    def phase_p3(g):
        for oblk in range(16):
            par = oblk % 2
            W("sync", sPE, slab_free[par])
            W("sync", sQ, wT_built)
            cs = slab[:, par * 8192:(par + 1) * 8192]
            c_cs = DMA("sync", sCOL[par], cs.rearrange("p (c o) -> p c o", o=256),
                       woT[:, oblk * 256:(oblk + 1) * 256]
                       .rearrange("(c p) o -> p c o", p=128))
            for rt in range(NRT):
                tile_i = oblk * NRT + rt
                slot = tile_i % 8
                pt = PA[:, slot * 512:slot * 512 + 256]
                W("tensor", sCOL[par], c_cs)
                W("tensor", sDVE, p3_psum_free.get(slot, 0))
                for cblk in range(32):
                    W("tensor", sDVE, ctx_ready.get(cblk, 0))
                    MM(pt, xcar[:, cblk * TG + rt * 128:cblk * TG + (rt + 1) * 128],
                       cs[:, cblk * 256:(cblk + 1) * 256],
                       start=(cblk == 0), stop=(cblk == 31))
                pe_mm = sPE.v
                W("vector", sPE, pe_mm)
                p3_psum_free[slot] = OP(
                    "vector", sDVE, "tensor_copy",
                    attnb[:, rt * D + oblk * 256:rt * D + (oblk + 1) * 256], pt)
            slab_free[par] = sPE.v

    def ln_finish(junk_bf):
        W("vector", sDVE, sDVE.v)
        OP("vector", sDVE, "tensor_scalar_mul", mean[:, :], mean[:, :], 1.0 / D)
        W("vector", sDVE, sDVE.v)
        OP("vector", sDVE, "tensor_scalar", z[:, :], z[:, :], mean[:, 0:1], None,
           op0=ALU.subtract)
        W("vector", sDVE, sDVE.v)
        OP("vector", sDVE, "scalar_tensor_tensor", junk_bf, z[:, :], 1.0, z[:, :],
           ALU.mult, ALU.mult, accum_out=varv[:, :])
        dve_var = sDVE.v
        W("scalar", sDVE, dve_var)
        W("scalar", sK, k_all)
        act_std = OP("scalar", sACT, "activation", stdv[:, :], varv[:, :], AF.Sqrt,
                     konst[:, 1:2], 1.0 / D)
        W("vector", sACT, act_std)
        OP("vector", sDVE, "reciprocal", rstd[:, :], stdv[:, :])
        W("vector", sDVE, sDVE.v)
        OP("vector", sDVE, "tensor_scalar", z[:, :], z[:, :], rstd[:, 0:1], None,
           op0=ALU.mult)
        return sDVE.v

    def phase_p3b(g):
        nonlocal last_zq, last_zg
        for rt in range(NRT):
            rows = slice(g * TG + rt * 128, g * TG + (rt + 1) * 128)
            W("sync", sQ, last_zq)
            W("sync", sGZ, last_zg)
            W("sync", sDVE, sDVE.v)
            c_x = DMA("sync", sZX, z[:, :], x_in[rows, :])
            W("vector", sZX, c_x)
            W("vector", sDVE, sDVE.v)
            OP("vector", sDVE, "scalar_tensor_tensor", z[:, :], z[:, :], 1.0,
               attnb[:, rt * D:(rt + 1) * D], ALU.mult, ALU.add,
               accum_out=mean[:, :])
            dve_x1 = ln_finish(attnb[:, rt * D:(rt + 1) * D])
            W("sync", sDVE, dve_x1)
            last_zq = DMA("sync", sQ, x1f[rows, :], z[:, :])
            W("gpsimd", sDVE, dve_x1)
            last_zg = DMA("gpsimd", sGZ, x1b[rows, :], z[:, :])

    def phase_p4():
        nonlocal last_zq, last_zg
        for c in range(4):
            DMA("gpsimd", sGF, kvw[:, c * 4096:(c + 1) * 4096],
                fk_in[c * 128:(c + 1) * 128, :])
        w_all = sGF.v
        x1b_all = last_zg
        for g in range(G):
            W("sync", sGZ, x1b_all)
            W("sync", sQ, last_zq)
            W("sync", sPE, sPE.v)
            xt_all = 0
            for ic in range(32):
                xt_all = DMA("sync", sXT, xcar[:, ic * TG:(ic + 1) * TG],
                             x1b[g * TG:(g + 1) * TG, ic * 128:(ic + 1) * 128],
                             transpose=True)
            for rt in range(NRT):
                rows = slice(g * TG + rt * 128, g * TG + (rt + 1) * 128)
                W("sync", sQ, last_zq)
                W("sync", sDVE, sDVE.v)
                c_x1 = DMA("sync", sZX, z[:, :], x1f[rows, :])
                accs = (mean, var2)
                for half in range(2):
                    pbase = half * 2048
                    ph = PA[:, pbase:pbase + 2048]
                    W("tensor", sXT, xt_all)
                    W("tensor", sGF, w_all)
                    W("tensor", sDVE, sDVE.v)
                    for ic in range(32):
                        for nn2 in range(4):
                            MM(ph[:, nn2 * 512:(nn2 + 1) * 512],
                               xcar[:, ic * TG + rt * 128:ic * TG + (rt + 1) * 128],
                               kvw[:, (ic % 4) * 4096 + pbase + nn2 * 512:
                                   (ic % 4) * 4096 + pbase + (nn2 + 1) * 512],
                               start=(ic == 0), stop=(ic == 31))
                    pe_h = sPE.v
                    W("vector", sPE, pe_h)
                    W("vector", sZX, c_x1)
                    OP("vector", sDVE, "scalar_tensor_tensor",
                       z[:, pbase:pbase + 2048], z[:, pbase:pbase + 2048], 1.0, ph,
                       ALU.mult, ALU.add, accum_out=accs[half][:, :])
                W("vector", sDVE, sDVE.v)
                OP("vector", sDVE, "tensor_add", mean[:, :], mean[:, :], var2[:, :])
                dve_y = ln_finish(attnb[:, 0:D])
                W("sync", sDVE, dve_y)
                last_zq = DMA("sync", sQ, y_out[rows, :], z[:, :])

    woT_built = False
    for g in range(G):
        compute_barrier()
        phase_p1(g)
        compute_barrier()
        phase_p2(g)
        if not woT_built:
            W("sync", sPE, sPE.v)
            wT_built = build_wT(wo16, woT, wo_cast_all)
            woT_built = True
        compute_barrier()
        phase_p3(g)
        compute_barrier()
        phase_p3b(g)
    compute_barrier()
    phase_p4()
    for c in dma_ctrs:
        W("sync", c, c.v)

    import contextlib
    es = contextlib.ExitStack()
    for n in names:
        C[n].h = es.enter_context(nc.semaphore(n))
    with nc.Block() as block:
        @block.sync
        def _(e):
            for f in prog["sync"]:
                f(e)

        @block.gpsimd
        def _(e):
            for f in prog["gpsimd"]:
                f(e)

        @block.tensor
        def _(e):
            for f in prog["tensor"]:
                f(e)

        @block.vector
        def _(e):
            for f in prog["vector"]:
                f(e)

        @block.scalar
        def _(e):
            for f in prog["scalar"]:
                f(e)
    es.close()
    n_inst = {k: len(v) for k, v in prog.items()}
    print("instructions per engine:", n_inst, "total:", sum(n_inst.values()))
    return nc


def _get_nc():
    global _NC_CACHE
    if _NC_CACHE is None:
        _NC_CACHE = build_nc()
    return _NC_CACHE


def kernel(L0_attn_hidden_states, L0_attn_k_cache, L0_attn_v_cache,
           L0_attn_Wq, L0_attn_Wo, L0_attn_block_table, L0_ffn_kernel):
    x = np.ascontiguousarray(np.asarray(L0_attn_hidden_states, dtype=np.float32)
                             .reshape(16384, D))
    wq = np.ascontiguousarray(np.asarray(L0_attn_Wq, dtype=np.float32))
    wo = np.ascontiguousarray(np.asarray(L0_attn_Wo, dtype=np.float32))
    kc = np.ascontiguousarray(np.asarray(L0_attn_k_cache, dtype=np.float32))
    vc = np.ascontiguousarray(np.asarray(L0_attn_v_cache, dtype=np.float32))
    bt = np.ascontiguousarray(np.asarray(L0_attn_block_table, dtype=np.int32))
    fk = np.ascontiguousarray(np.asarray(L0_ffn_kernel, dtype=np.int8).reshape(512, D))

    nc = _get_nc()
    in_maps = []
    for c in range(NCORES):
        cosT, sinT = _host_tables(c)
        in_maps.append({
            "x_in": np.ascontiguousarray(x[c * T:(c + 1) * T]),
            "wq_in": wq, "wo_in": wo, "kc_in": kc, "vc_in": vc,
            "bt_in": bt, "fk_in": fk,
            "cos_in": cosT, "sin_in": sinT,
        })
    res = run_bass_kernel_spmd(nc, in_maps, core_ids=list(range(NCORES)))
    out = np.concatenate([np.asarray(r["y_out"]) for r in res.results], axis=0)
    return out.reshape(4, 4096, 4096)


if __name__ == "__main__":
    build_nc()
    print("build ok")


# version 04 (final)
# speedup vs baseline: 1.6176x; 1.6176x over previous
import numpy as np
import ml_dtypes
import concourse.bass as bass
import concourse.mybir as mybir
from concourse.bass import IndirectOffsetOnAxis
from concourse.bass_utils import run_bass_kernel_spmd

F32 = mybir.dt.float32
BF16 = mybir.dt.bfloat16
I32 = mybir.dt.int32
I8 = mybir.dt.int8
AF = mybir.ActivationFunctionType
ALU = mybir.AluOpType

NCORES = 8
T = 2048
D = 4096
G = 4
TG = T // G
NRT = TG // 128
NH = 32
HD = 128
NKV = 8
NBLK, BS, NSEL = 512, 16, 256
SCALE = float(np.float32(1.0) / np.float32(np.sqrt(np.float32(128.0))))
CSHIFT = 10.0
EPS = 1e-5
SEQ = 4096


def _host_tables(core):
    inv_freq = (np.float32(1.0) /
                np.power(np.float32(10000.0),
                         np.arange(0, HD, 2, dtype=np.float32) / np.float32(HD)))
    t = np.arange(SEQ, dtype=np.float32)
    freqs = np.outer(t, inv_freq).astype(np.float32)
    emb = np.concatenate([freqs, freqs], -1)
    cos = np.cos(emb).astype(np.float32)
    sin = np.sin(emb).astype(np.float32)
    off = (core % 2) * T
    return (np.ascontiguousarray(cos[off:off + T].T),
            np.ascontiguousarray(sin[off:off + T].T))


def _prot():
    p = np.zeros((128, 128), dtype=np.float32)
    for s in range(64):
        p[s, s + 64] = 1.0
    for s in range(64, 128):
        p[s, s - 64] = -1.0
    return p.astype(ml_dtypes.bfloat16)


class _Ctr:
    __slots__ = ("h", "v", "name")

    def __init__(self, name):
        self.h = None
        self.v = 0
        self.name = name


_SCHED_META = None
_NC_CACHE = None


def build_nc():
    global _SCHED_META
    nc = bass.Bass("TRN2", target_bir_lowering=False)

    x_in = nc.dram_tensor("x_in", [T, D], F32, kind="ExternalInput").ap()
    wq_in = nc.dram_tensor("wq_in", [D, D], F32, kind="ExternalInput").ap()
    wo_in = nc.dram_tensor("wo_in", [D, D], F32, kind="ExternalInput").ap()
    kc_in = nc.dram_tensor("kc_in", [NBLK, NKV, BS, HD], F32, kind="ExternalInput").ap()
    vc_in = nc.dram_tensor("vc_in", [NBLK, NKV, BS, HD], F32, kind="ExternalInput").ap()
    bt_in = nc.dram_tensor("bt_in", [NSEL], I32, kind="ExternalInput").ap()
    fk_in = nc.dram_tensor("fk_in", [512, D], I8, kind="ExternalInput").ap()
    cos_in = nc.dram_tensor("cos_in", [HD, T], F32, kind="ExternalInput").ap()
    sin_in = nc.dram_tensor("sin_in", [HD, T], F32, kind="ExternalInput").ap()
    y_out = nc.dram_tensor("y_out", [T, D], F32, kind="ExternalOutput").ap()

    x16 = nc.dram_tensor("x16_s", [T, D], BF16, kind="Internal").ap()
    wq16 = nc.dram_tensor("wq16_s", [D, D], BF16, kind="Internal").ap()
    wo16 = nc.dram_tensor("wo16_s", [D, D], BF16, kind="Internal").ap()
    wqTs = nc.dram_tensor("wqT_s", [32, 128, 32, 128], BF16, kind="Internal").ap()
    woTs = nc.dram_tensor("woT_s", [16, 128, 32, 256], BF16, kind="Internal").ap()
    kg = nc.dram_tensor("kg_s", [NKV, NSEL, BS, HD], BF16, kind="Internal").ap()
    vgo = nc.dram_tensor("vgo_s", [NKV, NSEL, BS, HD], BF16, kind="Internal").ap()
    vg = nc.dram_tensor("vg_s", [NKV, 128, 32, HD], BF16, kind="Internal").ap()
    x1f = nc.dram_tensor("x1f_s", [T, D], F32, kind="Internal").ap()
    x1b = nc.dram_tensor("x1b_s", [T, D], BF16, kind="Internal").ap()

    ones_col = nc.inline_tensor(np.ones((128, 1), dtype=np.float32), name="ones_col").ap()
    ones_row = nc.inline_tensor(np.ones((1, 128), dtype=np.float32), name="ones_row").ap()
    prot_c = nc.inline_tensor(_prot(), name="prot_c").ap()
    konst_c = nc.inline_tensor(
        np.tile(np.array([[-CSHIFT, EPS, 0.0, 0.0]], dtype=np.float32), (128, 1)),
        name="konst_c").ap()

    sbt = nc.alloc_sbuf_tensor
    qTar = sbt("qTar", [128, NH * TG], BF16)
    xcar = sbt("xcar", [128, 32 * TG], BF16)
    kvw = sbt("kvw", [128, 16384], BF16)
    slab = sbt("slab", [128, 16384], BF16)
    ebuf = sbt("ebuf", [128, 3 * 1024], BF16)
    acc = sbt("acc", [128, 2 * 1024], F32)
    attnb = sbt("attnb", [128, NRT * D], BF16)
    z = sbt("z", [128, D], F32)
    qraw = sbt("qraw", [128, 2 * TG], BF16)
    cosg = sbt("cosg", [128, TG], F32)
    sing = sbt("sing", [128, TG], F32)
    idx = sbt("idx", [128, 2], I32)
    onec = sbt("onec", [128, 1], F32)
    oner = sbt("oner", [1, 128], F32)
    prot = sbt("prot", [128, 128], BF16)
    konst = sbt("konst", [128, 4], F32)
    recip = sbt("recip", [1, 1024], F32)
    mean = sbt("mean", [128, 1], F32)
    var2 = sbt("var2", [128, 1], F32)
    varv = sbt("varv", [128, 1], F32)
    stdv = sbt("stdv", [128, 1], F32)
    rstd = sbt("rstd", [128, 1], F32)

    PA = nc.alloc_psum_tensor("PA", [128, 4096], F32)

    prog = {k: [] for k in ("sync", "gpsimd", "tensor", "vector", "scalar")}
    waited = {k: {} for k in prog}
    _SCHED_META = {k: [] for k in prog}

    names = ("sQ", "sK", "sGSt", "sXT", "sCS", "sJS0", "sJS1", "sKV0", "sKV1",
             "sCOL0", "sCOL1", "sZX", "sGX", "sGX0", "sGG", "sGW", "sGZ", "sGF",
             "sVR", "sWB0", "sWB1", "sWB2", "sWB3",
             "sPE", "sDVE", "sACT", "sPL")
    C = {n: _Ctr(n) for n in names}
    sQ, sK, sGSt, sXT, sCS = C["sQ"], C["sK"], C["sGSt"], C["sXT"], C["sCS"]
    sJS = [C["sJS0"], C["sJS1"]]
    sKV = [C["sKV0"], C["sKV1"]]
    sCOL = [C["sCOL0"], C["sCOL1"]]
    sZX, sGX, sGX0, sGG, sGW, sGZ, sGF = (C["sZX"], C["sGX"], C["sGX0"],
                                          C["sGG"], C["sGW"], C["sGZ"], C["sGF"])
    sPE, sDVE, sACT, sPL = C["sPE"], C["sDVE"], C["sACT"], C["sPL"]
    sWB = [C["sWB0"], C["sWB1"], C["sWB2"], C["sWB3"]]
    sVR = C["sVR"]
    dma_ctrs = [sQ, sK, sGSt, sXT, sCS, sJS[0], sJS[1], sKV[0], sKV[1],
                sCOL[0], sCOL[1], sZX, sGX, sGX0, sGG, sGW, sGZ, sGF, sVR] + sWB

    def E(eng, fn):
        prog[eng].append(fn)

    def W(eng, ctr, val):
        if val <= 0:
            return
        if val <= waited[eng].get(ctr.name, 0):
            return
        waited[eng][ctr.name] = val
        _SCHED_META[eng].append(("w", ctr.name, val))
        E(eng, lambda e, c=ctr, v=val: e.wait_ge(c.h, v))

    def DMA(qname, ctr, out, in_, **kw):
        ctr.v += 16
        _SCHED_META[qname].append(("i", ctr.name, 16))
        E(qname, lambda e, o=out, i=in_, c=ctr, kw2=dict(kw):
          e.dma_start(out=o, in_=i, **kw2).then_inc(c.h, 16))
        return ctr.v

    def OP(eng, ctr, meth, *args, **kw):
        ctr.v += 1
        _SCHED_META[eng].append(("i", ctr.name, 1))
        E(eng, lambda e, m=meth, a=args, c=ctr, kw2=dict(kw):
          getattr(e, m)(*a, **kw2).then_inc(c.h, 1))
        return ctr.v

    def MM(out, lhsT, rhs, start, stop):
        sPE.v += 1
        _SCHED_META["tensor"].append(("i", "sPE", 1))
        E("tensor", lambda e, o=out, l=lhsT, r=rhs, st=start, sp=stop, c=sPE:
          e.matmul(o, l, r, start=st, stop=sp).then_inc(c.h, 1))
        return sPE.v

    def compute_barrier():
        for eng in ("tensor", "vector", "scalar"):
            for c in (sPE, sDVE, sACT):
                W(eng, c, c.v)

    DMA("sync", sK, onec[:, :], ones_col)
    DMA("sync", sK, oner[:, :], ones_row)
    DMA("sync", sK, prot[:, :], prot_c)
    DMA("sync", sK, konst[:, :], konst_c)
    k_all = DMA("sync", sK, idx[:, :], bt_in.rearrange("(b p) -> p b", p=128),
                allow_slow_non_contiguous=True)

    DMA("gpsimd", sGX0, x16[0:TG, :], x_in[0:TG, :])
    x0_cast = sGX0.v
    for i in range(4):
        DMA("gpsimd", sGW, wq16[i * 1024:(i + 1) * 1024, :],
            wq_in[i * 1024:(i + 1) * 1024, :])
    wq_cast_all = sGW.v
    for i in range(1, G):
        DMA("gpsimd", sGX, x16[i * TG:(i + 1) * TG, :], x_in[i * TG:(i + 1) * TG, :])
    x_cast_all = sGX.v

    kc_flat = kc_in.rearrange("b h r d -> b (h r d)")
    vc_flat = vc_in.rearrange("b h r d -> b (h r d)")
    slab_g3 = slab[:, :].rearrange("p (h rd) -> p h rd", rd=BS * HD)
    gath_done = 0
    for (src, dst) in ((kc_flat, kg), (vc_flat, vgo)):
        for b in range(2):
            W("gpsimd", sK, k_all)
            W("gpsimd", sGSt, gath_done)
            sGG.v += 16
            _SCHED_META["gpsimd"].append(("i", "sGG", 16))
            E("gpsimd", lambda e, s=src, bb=b: e.indirect_dma_start(
                slab[:, :], None, s,
                IndirectOffsetOnAxis(idx[:, bb:bb + 1], 0)).then_inc(sGG.h, 16))
            W("sync", sGG, sGG.v)
            dst4 = dst[:, b * 128:(b + 1) * 128, :, :].rearrange("h i r d -> i h (r d)")
            gath_done = DMA("sync", sGSt, dst4, slab_g3)
    gath_all = gath_done
    W("sync", sGSt, gath_all)
    for hk in range(NKV):
        DMA("sync", sVR, vg[hk, :, :, :],
            vgo[hk, :, :, :].rearrange("(c a) r d -> (a r) c d", a=8))
    vr_all = sVR.v

    W("gpsimd", sGW, wq_cast_all)
    for i in range(4):
        DMA("gpsimd", sGW, wo16[i * 1024:(i + 1) * 1024, :],
            wo_in[i * 1024:(i + 1) * 1024, :])
    wo_cast_all = sGW.v

    wb_t = [0, 0, 0, 0]
    wb_s = [0, 0, 0, 0]

    def build_wT(w16_ap, wTs_ap, obw, cast_val):
        stiles = [slab[:, s * 4096:(s + 1) * 4096] for s in range(4)]
        nob = 4096 // obw

        def emit_store(dcx):
            s = dcx % 4
            W("sync", sWB[s], wb_t[s])
            dst = wTs_ap[:, :, dcx, :].rearrange("ob p o -> p ob o")
            src = stiles[s].rearrange("p (ob o) -> p ob o", o=obw)
            wb_s[s] = DMA("sync", sQ, dst, src)

        W("sync", sGW, cast_val)
        for dc in range(32):
            s = dc % 4
            if wb_s[s]:
                W("sync", sQ, sQ.v)
            wb_t[s] = DMA("sync", sWB[s], stiles[s],
                          w16_ap[:, dc * 128:(dc + 1) * 128], transpose=True)
            if dc >= 1:
                emit_store(dc - 1)
        emit_store(31)
        return max(wb_s)

    W("sync", sGSt, gath_all)
    wT_built = build_wT(wq16, wqTs, 128, wq_cast_all)

    ebuf_free_pe = [0, 0, 0]
    ebuf_free_acc = [(None, 0), (None, 0), (None, 0)]
    qpsum_free = [0, 0]
    rpsum_free = [0, 0]
    qraw_free_pe = [0, 0]
    qraw_free_dve = [0, 0]
    spsum_free = [0, 0]
    kv_free = [0, 0]
    slab_free = [0, 0]
    qTar_ready = {}
    ctx_ready = {}
    p3_psum_free = {}
    ctx_evac_done = 0
    tail_free_dve = 0
    acc_guard = [0, 0]
    pe_sums_prev = 0
    last_zq = 0
    last_zg = 0

    def phase_p1(g):
        W("sync", sPE, sPE.v)
        W("sync", sDVE, sDVE.v)
        W("vector", sGZ, last_zg)
        W("vector", sQ, last_zq)
        DMA("sync", sCS, cosg[:, :], cos_in[:, g * TG:(g + 1) * TG])
        cs_all = DMA("sync", sCS, sing[:, :], sin_in[:, g * TG:(g + 1) * TG])
        if g == 0:
            W("sync", sGX0, x0_cast)
        else:
            W("sync", sGX, x_cast_all)
        xt_all = 0
        for dc in range(32):
            xt_all = DMA("sync", sXT, xcar[:, dc * TG:(dc + 1) * TG],
                         x16[g * TG:(g + 1) * TG, dc * 128:(dc + 1) * 128],
                         transpose=True)
        for jblk in range(NH):
            par = jblk % 2
            W("sync", sQ, wT_built)
            W("sync", sPE, slab_free[par])
            js = slab[:, par * 4096:(par + 1) * 4096]
            c_js = DMA("sync", sJS[par], js.rearrange("p (dc j) -> p dc j", j=128),
                       wqTs[jblk, :, :, :])
            qp = PA[:, par * TG:(par + 1) * TG]
            W("tensor", sJS[par], c_js)
            W("tensor", sXT, xt_all)
            W("tensor", sACT, qpsum_free[par])
            for dc in range(32):
                MM(qp, js[:, dc * 128:(dc + 1) * 128], xcar[:, dc * TG:(dc + 1) * TG],
                   start=(dc == 0), stop=(dc == 31))
            slab_free[par] = sPE.v
            pe_q = sPE.v
            W("scalar", sPE, pe_q)
            W("scalar", sPE, qraw_free_pe[par])
            W("scalar", sDVE, qraw_free_dve[par])
            W("scalar", sACT, qpsum_free[par])
            qpsum_free[par] = OP("scalar", sACT, "copy",
                                 qraw[:, par * TG:(par + 1) * TG], qp)
            rp = PA[:, 2048 + par * TG: 2048 + (par + 1) * TG]
            W("tensor", sACT, qpsum_free[par])
            W("tensor", sDVE, rpsum_free[par])
            pe_rot = MM(rp, prot[:, :], qraw[:, par * TG:(par + 1) * TG],
                        start=True, stop=True)
            qraw_free_pe[par] = pe_rot
            W("vector", sCS, cs_all)
            W("vector", sPE, pe_rot)
            W("vector", sACT, qpsum_free[par])
            t1 = z[:, 1024 + par * TG:1024 + (par + 1) * TG]
            OP("vector", sDVE, "tensor_mul", t1, rp, sing[:, :])
            rpsum_free[par] = sDVE.v
            OP("vector", sDVE, "tensor_mul", z[:, par * TG:(par + 1) * TG],
               qraw[:, par * TG:(par + 1) * TG], cosg[:, :])
            qraw_free_dve[par] = sDVE.v
            W("vector", sDVE, sDVE.v)
            qTar_ready[jblk] = OP("vector", sDVE, "tensor_add",
                                  qTar[:, jblk * TG:(jblk + 1) * TG],
                                  z[:, par * TG:(par + 1) * TG], t1)

    def phase_p2(g):
        nonlocal ctx_evac_done, tail_free_dve, pe_sums_prev
        W("scalar", sGZ, last_zg)
        W("scalar", sQ, last_zq)
        for hk in range(NKV):
            par = hk % 2
            W("sync", sGSt, gath_all)
            W("sync", sVR, vr_all)
            W("sync", sPE, kv_free[par])
            base = par * 8192
            DMA("sync", sKV[par], kvw[:, base:base + 4096],
                kg[hk, :, :, :].rearrange("i r d -> (i r) d"), transpose=True)
            kv_ready = DMA(
                "sync", sKV[par],
                kvw[:, base + 4096:base + 8192],
                vg[hk, :, :, :].rearrange("p c d -> p (c d)"))
            for pr in range(2):
                h0 = hk * 4 + 2 * pr
                h1 = h0 + 1
                W("tensor", sKV[par], kv_ready)
                W("tensor", sDVE, max(qTar_ready.get(h0, 0), qTar_ready.get(h1, 0)))
                W("vector", sPE, pe_sums_prev)
                W("gpsimd", sPE, pe_sums_prev)
                acc_last = [0, 0]

                def emit_scores(j):
                    sp = j % 2
                    scp = PA[:, sp * 1024:(sp + 1) * 1024]
                    kchunk = kvw[:, base + j * 128:base + (j + 1) * 128]
                    W("tensor", sACT, spsum_free[sp])
                    MM(scp[:, 0:TG], kchunk, qTar[:, h0 * TG:(h0 + 1) * TG],
                       start=True, stop=True)
                    return MM(scp[:, TG:2 * TG], kchunk,
                              qTar[:, h1 * TG:(h1 + 1) * TG], start=True, stop=True)

                pe_sc = {0: emit_scores(0)}
                for j in range(32):
                    sp = j % 2
                    eslot = j % 3
                    if j + 1 < 32:
                        pe_sc[j + 1] = emit_scores(j + 1)
                    scp = PA[:, sp * 1024:(sp + 1) * 1024]
                    eb = ebuf[:, eslot * 1024:(eslot + 1) * 1024]
                    W("scalar", sPE, max(pe_sc[j], ebuf_free_pe[eslot]))
                    en, ev = ebuf_free_acc[eslot]
                    if en is not None:
                        W("scalar", en, ev)
                    W("scalar", sK, k_all)
                    act_e = OP("scalar", sACT, "activation", eb, scp, AF.Exp,
                               konst[:, 0:1], SCALE)
                    spsum_free[sp] = act_e
                    aeng = "vector" if sp == 0 else "gpsimd"
                    actr = sDVE if sp == 0 else sPL
                    W(aeng, sACT, act_e)
                    W(aeng, actr, acc_guard[sp])
                    if j < 2:
                        acc_last[j] = OP(aeng, actr, "tensor_copy",
                                         acc[:, sp * 1024:(sp + 1) * 1024], eb)
                    else:
                        acc_last[sp] = OP(aeng, actr, "tensor_add",
                                          acc[:, sp * 1024:(sp + 1) * 1024],
                                          acc[:, sp * 1024:(sp + 1) * 1024], eb)
                    acc_guard[sp] = acc_last[sp]
                    ebuf_free_acc[eslot] = (actr, acc_last[sp])
                    vchunk = kvw[:, base + 4096 + j * 128:base + 4096 + (j + 1) * 128]
                    W("tensor", sACT, act_e)
                    if j == 0:
                        W("tensor", sDVE, ctx_evac_done)
                    MM(PA[:, 2048:2048 + TG], vchunk, eb[:, 0:TG],
                       start=(j == 0), stop=(j == 31))
                    pe_ctx = MM(PA[:, 2048 + TG:2048 + 2 * TG], vchunk,
                                eb[:, TG:2 * TG], start=(j == 0), stop=(j == 31))
                    ebuf_free_pe[eslot] = pe_ctx
                pe_ctx_last = sPE.v
                W("tensor", sDVE, acc_last[0])
                W("tensor", sPL, acc_last[1])
                W("tensor", sDVE, tail_free_dve)
                for half in range(2):
                    tail = PA[0:1, 3072 + half * 512:3072 + (half + 1) * 512]
                    MM(tail, onec[:, :], acc[:, half * 512:(half + 1) * 512],
                       start=True, stop=False)
                    MM(tail, onec[:, :],
                       acc[:, 1024 + half * 512:1024 + (half + 1) * 512],
                       start=False, stop=True)
                pe_sums = sPE.v
                pe_sums_prev = pe_sums
                W("vector", sPE, pe_sums)
                dve_recip = OP("vector", sDVE, "reciprocal", recip[:, :],
                               PA[0:1, 3072:4096])
                W("tensor", sDVE, dve_recip)
                for half in range(2):
                    MM(PA[:, 3072 + half * 512:3072 + (half + 1) * 512], oner[:, :],
                       recip[0:1, half * 512:(half + 1) * 512], start=True, stop=True)
                pe_bc = sPE.v
                W("vector", sPE, pe_bc)
                rbc_val = OP("vector", sDVE, "tensor_copy", z[:, 0:1024],
                             PA[:, 3072:4096])
                tail_free_dve = rbc_val
                W("vector", sPE, pe_ctx_last)
                W("vector", sDVE, rbc_val)
                OP("vector", sDVE, "tensor_mul", xcar[:, h0 * TG:(h0 + 1) * TG],
                   PA[:, 2048:2048 + TG], z[:, 0:TG])
                ctx_ready[h0] = sDVE.v
                OP("vector", sDVE, "tensor_mul", xcar[:, h1 * TG:(h1 + 1) * TG],
                   PA[:, 2048 + TG:2048 + 2 * TG], z[:, TG:2 * TG])
                ctx_ready[h1] = sDVE.v
                ctx_evac_done = sDVE.v
            kv_free[par] = sPE.v

    def phase_p3(g):
        for oblk in range(16):
            par = oblk % 2
            W("sync", sPE, slab_free[par])
            W("sync", sQ, wT_built)
            cs = slab[:, par * 8192:(par + 1) * 8192]
            c_cs = DMA("sync", sCOL[par], cs.rearrange("p (c o) -> p c o", o=256),
                       woTs[oblk, :, :, :])
            for rt in range(NRT):
                tile_i = oblk * NRT + rt
                slot = tile_i % 8
                pt = PA[:, slot * 512:slot * 512 + 256]
                W("tensor", sCOL[par], c_cs)
                W("tensor", sDVE, p3_psum_free.get(slot, 0))
                for cblk in range(32):
                    W("tensor", sDVE, ctx_ready.get(cblk, 0))
                    MM(pt, xcar[:, cblk * TG + rt * 128:cblk * TG + (rt + 1) * 128],
                       cs[:, cblk * 256:(cblk + 1) * 256],
                       start=(cblk == 0), stop=(cblk == 31))
                pe_mm = sPE.v
                W("vector", sPE, pe_mm)
                p3_psum_free[slot] = OP(
                    "vector", sDVE, "tensor_copy",
                    attnb[:, rt * D + oblk * 256:rt * D + (oblk + 1) * 256], pt)
            slab_free[par] = sPE.v

    def ln_finish(junk_bf):
        W("vector", sDVE, sDVE.v)
        OP("vector", sDVE, "tensor_scalar_mul", mean[:, :], mean[:, :], 1.0 / D)
        W("vector", sDVE, sDVE.v)
        OP("vector", sDVE, "tensor_scalar", z[:, :], z[:, :], mean[:, 0:1], None,
           op0=ALU.subtract)
        W("vector", sDVE, sDVE.v)
        OP("vector", sDVE, "scalar_tensor_tensor", junk_bf, z[:, :], 1.0, z[:, :],
           ALU.mult, ALU.mult, accum_out=varv[:, :])
        dve_var = sDVE.v
        W("scalar", sDVE, dve_var)
        W("scalar", sK, k_all)
        act_std = OP("scalar", sACT, "activation", stdv[:, :], varv[:, :], AF.Sqrt,
                     konst[:, 1:2], 1.0 / D)
        W("vector", sACT, act_std)
        OP("vector", sDVE, "reciprocal", rstd[:, :], stdv[:, :])
        W("vector", sDVE, sDVE.v)
        OP("vector", sDVE, "tensor_scalar", z[:, :], z[:, :], rstd[:, 0:1], None,
           op0=ALU.mult)
        return sDVE.v

    def phase_p3b(g):
        nonlocal last_zq, last_zg
        for rt in range(NRT):
            rows = slice(g * TG + rt * 128, g * TG + (rt + 1) * 128)
            W("sync", sQ, last_zq)
            W("sync", sGZ, last_zg)
            W("sync", sDVE, sDVE.v)
            c_x = DMA("sync", sZX, z[:, :], x_in[rows, :])
            W("vector", sZX, c_x)
            W("vector", sDVE, sDVE.v)
            OP("vector", sDVE, "scalar_tensor_tensor", z[:, :], z[:, :], 1.0,
               attnb[:, rt * D:(rt + 1) * D], ALU.mult, ALU.add,
               accum_out=mean[:, :])
            dve_x1 = ln_finish(attnb[:, rt * D:(rt + 1) * D])
            W("sync", sDVE, dve_x1)
            last_zq = DMA("sync", sQ, x1f[rows, :], z[:, :])
            W("gpsimd", sDVE, dve_x1)
            last_zg = DMA("gpsimd", sGZ, x1b[rows, :], z[:, :])

    def phase_p4():
        nonlocal last_zq, last_zg
        p4_free = [0, 0]
        for c in range(4):
            DMA("gpsimd", sGF, kvw[:, c * 4096:(c + 1) * 4096],
                fk_in[c * 128:(c + 1) * 128, :])
        w_all = sGF.v
        x1b_all = last_zg
        for g in range(G):
            W("sync", sGZ, x1b_all)
            W("sync", sQ, last_zq)
            W("sync", sPE, sPE.v)
            xt_all = 0
            for ic in range(32):
                xt_all = DMA("sync", sXT, xcar[:, ic * TG:(ic + 1) * TG],
                             x1b[g * TG:(g + 1) * TG, ic * 128:(ic + 1) * 128],
                             transpose=True)
            for rt in range(NRT):
                rows = slice(g * TG + rt * 128, g * TG + (rt + 1) * 128)
                W("sync", sQ, last_zq)
                W("sync", sDVE, sDVE.v)
                c_x1 = DMA("sync", sZX, z[:, :], x1f[rows, :])
                accs = (mean, var2)
                for half in range(2):
                    pbase = half * 2048
                    ph = PA[:, pbase:pbase + 2048]
                    W("tensor", sXT, xt_all)
                    W("tensor", sGF, w_all)
                    W("tensor", sDVE, p4_free[half])
                    for ic in range(32):
                        for nn2 in range(4):
                            MM(ph[:, nn2 * 512:(nn2 + 1) * 512],
                               xcar[:, ic * TG + rt * 128:ic * TG + (rt + 1) * 128],
                               kvw[:, (ic % 4) * 4096 + pbase + nn2 * 512:
                                   (ic % 4) * 4096 + pbase + (nn2 + 1) * 512],
                               start=(ic == 0), stop=(ic == 31))
                    pe_h = sPE.v
                    W("vector", sPE, pe_h)
                    W("vector", sZX, c_x1)
                    OP("vector", sDVE, "scalar_tensor_tensor",
                       z[:, pbase:pbase + 2048], z[:, pbase:pbase + 2048], 1.0, ph,
                       ALU.mult, ALU.add, accum_out=accs[half][:, :])
                    p4_free[half] = sDVE.v
                W("vector", sDVE, sDVE.v)
                OP("vector", sDVE, "tensor_add", mean[:, :], mean[:, :], var2[:, :])
                dve_y = ln_finish(attnb[:, 0:D])
                W("sync", sDVE, dve_y)
                last_zq = DMA("sync", sQ, y_out[rows, :], z[:, :])

    woT_built = False
    for g in range(G):
        compute_barrier()
        phase_p1(g)
        compute_barrier()
        phase_p2(g)
        if not woT_built:
            W("sync", sPE, sPE.v)
            wT_built = build_wT(wo16, woTs, 256, wo_cast_all)
            woT_built = True
        compute_barrier()
        phase_p3(g)
        compute_barrier()
        phase_p3b(g)
    compute_barrier()
    phase_p4()
    for c in dma_ctrs:
        W("sync", c, c.v)

    import contextlib
    es = contextlib.ExitStack()
    for n in names:
        C[n].h = es.enter_context(nc.semaphore(n))
    with nc.Block() as block:
        @block.sync
        def _(e):
            for f in prog["sync"]:
                f(e)

        @block.gpsimd
        def _(e):
            for f in prog["gpsimd"]:
                f(e)

        @block.tensor
        def _(e):
            for f in prog["tensor"]:
                f(e)

        @block.vector
        def _(e):
            for f in prog["vector"]:
                f(e)

        @block.scalar
        def _(e):
            for f in prog["scalar"]:
                f(e)
    es.close()
    n_inst = {k: len(v) for k, v in prog.items()}
    print("instructions per engine:", n_inst, "total:", sum(n_inst.values()))
    return nc


def _get_nc():
    global _NC_CACHE
    if _NC_CACHE is None:
        _NC_CACHE = build_nc()
    return _NC_CACHE


def kernel(L0_attn_hidden_states, L0_attn_k_cache, L0_attn_v_cache,
           L0_attn_Wq, L0_attn_Wo, L0_attn_block_table, L0_ffn_kernel):
    x = np.ascontiguousarray(np.asarray(L0_attn_hidden_states, dtype=np.float32)
                             .reshape(16384, D))
    wq = np.ascontiguousarray(np.asarray(L0_attn_Wq, dtype=np.float32))
    wo = np.ascontiguousarray(np.asarray(L0_attn_Wo, dtype=np.float32))
    kc = np.ascontiguousarray(np.asarray(L0_attn_k_cache, dtype=np.float32))
    vc = np.ascontiguousarray(np.asarray(L0_attn_v_cache, dtype=np.float32))
    bt = np.ascontiguousarray(np.asarray(L0_attn_block_table, dtype=np.int32))
    fk = np.ascontiguousarray(np.asarray(L0_ffn_kernel, dtype=np.int8).reshape(512, D))

    nc = _get_nc()
    in_maps = []
    for c in range(NCORES):
        cosT, sinT = _host_tables(c)
        in_maps.append({
            "x_in": np.ascontiguousarray(x[c * T:(c + 1) * T]),
            "wq_in": wq, "wo_in": wo, "kc_in": kc, "vc_in": vc,
            "bt_in": bt, "fk_in": fk,
            "cos_in": cosT, "sin_in": sinT,
        })
    res = run_bass_kernel_spmd(nc, in_maps, core_ids=list(range(NCORES)))
    out = np.concatenate([np.asarray(r["y_out"]) for r in res.results], axis=0)
    return out.reshape(4, 4096, 4096)


if __name__ == "__main__":
    build_nc()
    print("build ok")


# version 05
# speedup vs baseline: 1.6358x; 1.0112x over previous
import numpy as np
import ml_dtypes
import concourse.bass as bass
import concourse.mybir as mybir
from concourse.bass import IndirectOffsetOnAxis
from concourse.bass_utils import run_bass_kernel_spmd

F32 = mybir.dt.float32
BF16 = mybir.dt.bfloat16
I32 = mybir.dt.int32
I8 = mybir.dt.int8
AF = mybir.ActivationFunctionType
ALU = mybir.AluOpType

NCORES = 8
T = 2048
D = 4096
G = 4
TG = T // G
NRT = TG // 128
NH = 32
HD = 128
NKV = 8
NBLK, BS, NSEL = 512, 16, 256
SCALE = float(np.float32(1.0) / np.float32(np.sqrt(np.float32(128.0))))
CSHIFT = 10.0
EPS = 1e-5
SEQ = 4096


def _host_tables(core):
    inv_freq = (np.float32(1.0) /
                np.power(np.float32(10000.0),
                         np.arange(0, HD, 2, dtype=np.float32) / np.float32(HD)))
    t = np.arange(SEQ, dtype=np.float32)
    freqs = np.outer(t, inv_freq).astype(np.float32)
    emb = np.concatenate([freqs, freqs], -1)
    cos = np.cos(emb).astype(np.float32)
    sin = np.sin(emb).astype(np.float32)
    off = (core % 2) * T
    return (np.ascontiguousarray(cos[off:off + T].T),
            np.ascontiguousarray(sin[off:off + T].T))


def _prot():
    p = np.zeros((128, 128), dtype=np.float32)
    for s in range(64):
        p[s, s + 64] = 1.0
    for s in range(64, 128):
        p[s, s - 64] = -1.0
    return p.astype(ml_dtypes.bfloat16)


class _Ctr:
    __slots__ = ("h", "v", "name")

    def __init__(self, name):
        self.h = None
        self.v = 0
        self.name = name


_SCHED_META = None
_NC_CACHE = None


def build_nc():
    global _SCHED_META
    nc = bass.Bass("TRN2", target_bir_lowering=False)

    x_in = nc.dram_tensor("x_in", [T, D], F32, kind="ExternalInput").ap()
    wq_in = nc.dram_tensor("wq_in", [D, D], F32, kind="ExternalInput").ap()
    wo_in = nc.dram_tensor("wo_in", [D, D], F32, kind="ExternalInput").ap()
    kc_in = nc.dram_tensor("kc_in", [NBLK, NKV, BS, HD], F32, kind="ExternalInput").ap()
    vc_in = nc.dram_tensor("vc_in", [NBLK, NKV, BS, HD], F32, kind="ExternalInput").ap()
    bt_in = nc.dram_tensor("bt_in", [NSEL], I32, kind="ExternalInput").ap()
    fk_in = nc.dram_tensor("fk_in", [512, D], I8, kind="ExternalInput").ap()
    cos_in = nc.dram_tensor("cos_in", [HD, T], F32, kind="ExternalInput").ap()
    sin_in = nc.dram_tensor("sin_in", [HD, T], F32, kind="ExternalInput").ap()
    y_out = nc.dram_tensor("y_out", [T, D], F32, kind="ExternalOutput").ap()

    x16 = nc.dram_tensor("x16_s", [T, D], BF16, kind="Internal").ap()
    wq16 = nc.dram_tensor("wq16_s", [D, D], BF16, kind="Internal").ap()
    wo16 = nc.dram_tensor("wo16_s", [D, D], BF16, kind="Internal").ap()
    wqTs = nc.dram_tensor("wqT_s", [32, 128, 32, 128], BF16, kind="Internal").ap()
    woTs = nc.dram_tensor("woT_s", [16, 128, 32, 256], BF16, kind="Internal").ap()
    kg = nc.dram_tensor("kg_s", [NKV, NSEL, BS, HD], BF16, kind="Internal").ap()
    vgo = nc.dram_tensor("vgo_s", [NKV, NSEL, BS, HD], BF16, kind="Internal").ap()
    vg = nc.dram_tensor("vg_s", [NKV, 128, 32, HD], BF16, kind="Internal").ap()
    x1f = nc.dram_tensor("x1f_s", [T, D], F32, kind="Internal").ap()
    x1b = nc.dram_tensor("x1b_s", [T, D], BF16, kind="Internal").ap()

    ones_col = nc.inline_tensor(np.ones((128, 1), dtype=np.float32), name="ones_col").ap()
    ones_row = nc.inline_tensor(np.ones((1, 128), dtype=np.float32), name="ones_row").ap()
    prot_c = nc.inline_tensor(_prot(), name="prot_c").ap()
    konst_c = nc.inline_tensor(
        np.tile(np.array([[-CSHIFT, EPS, 0.0, 0.0]], dtype=np.float32), (128, 1)),
        name="konst_c").ap()

    sbt = nc.alloc_sbuf_tensor
    qTar = sbt("qTar", [128, NH * TG], BF16)
    xcar = sbt("xcar", [128, 32 * TG], BF16)
    kvw = sbt("kvw", [128, 16384], BF16)
    slab = sbt("slab", [128, 16384], BF16)
    ebuf = sbt("ebuf", [128, 3 * 1024], BF16)
    acc = sbt("acc", [128, 2 * 1024], F32)
    accg = sbt("accg", [128, 1024], F32)
    attnb = sbt("attnb", [128, NRT * D], BF16)
    z = sbt("z", [128, D], F32)
    qraw = sbt("qraw", [128, 2 * TG], BF16)
    cosg = sbt("cosg", [128, TG], F32)
    sing = sbt("sing", [128, TG], F32)
    idx = sbt("idx", [128, 2], I32)
    onec = sbt("onec", [128, 1], F32)
    oner = sbt("oner", [1, 128], F32)
    prot = sbt("prot", [128, 128], BF16)
    konst = sbt("konst", [128, 4], F32)
    recip = sbt("recip", [1, 1024], F32)
    mean = sbt("mean", [128, 1], F32)
    var2 = sbt("var2", [128, 1], F32)
    varv = sbt("varv", [128, 1], F32)
    stdv = sbt("stdv", [128, 1], F32)
    rstd = sbt("rstd", [128, 1], F32)

    PA = nc.alloc_psum_tensor("PA", [128, 4096], F32)

    prog = {k: [] for k in ("sync", "gpsimd", "tensor", "vector", "scalar")}
    waited = {k: {} for k in prog}
    _SCHED_META = {k: [] for k in prog}

    names = ("sQ", "sK", "sGSt", "sXT", "sCS", "sJS0", "sJS1", "sKV0", "sKV1",
             "sCOL0", "sCOL1", "sZX", "sGX", "sGX0", "sGG", "sGW", "sGZ", "sGF",
             "sVR", "sWB0", "sWB1", "sWB2", "sWB3",
             "sPE", "sDVE", "sACT", "sPL")
    C = {n: _Ctr(n) for n in names}
    sQ, sK, sGSt, sXT, sCS = C["sQ"], C["sK"], C["sGSt"], C["sXT"], C["sCS"]
    sJS = [C["sJS0"], C["sJS1"]]
    sKV = [C["sKV0"], C["sKV1"]]
    sCOL = [C["sCOL0"], C["sCOL1"]]
    sZX, sGX, sGX0, sGG, sGW, sGZ, sGF = (C["sZX"], C["sGX"], C["sGX0"],
                                          C["sGG"], C["sGW"], C["sGZ"], C["sGF"])
    sPE, sDVE, sACT, sPL = C["sPE"], C["sDVE"], C["sACT"], C["sPL"]
    sWB = [C["sWB0"], C["sWB1"], C["sWB2"], C["sWB3"]]
    sVR = C["sVR"]
    dma_ctrs = [sQ, sK, sGSt, sXT, sCS, sJS[0], sJS[1], sKV[0], sKV[1],
                sCOL[0], sCOL[1], sZX, sGX, sGX0, sGG, sGW, sGZ, sGF, sVR] + sWB

    def E(eng, fn):
        prog[eng].append(fn)

    def W(eng, ctr, val):
        if val <= 0:
            return
        if val <= waited[eng].get(ctr.name, 0):
            return
        waited[eng][ctr.name] = val
        _SCHED_META[eng].append(("w", ctr.name, val))
        E(eng, lambda e, c=ctr, v=val: e.wait_ge(c.h, v))

    def DMA(qname, ctr, out, in_, **kw):
        ctr.v += 16
        _SCHED_META[qname].append(("i", ctr.name, 16))
        E(qname, lambda e, o=out, i=in_, c=ctr, kw2=dict(kw):
          e.dma_start(out=o, in_=i, **kw2).then_inc(c.h, 16))
        return ctr.v

    def OP(eng, ctr, meth, *args, **kw):
        ctr.v += 1
        _SCHED_META[eng].append(("i", ctr.name, 1))
        E(eng, lambda e, m=meth, a=args, c=ctr, kw2=dict(kw):
          getattr(e, m)(*a, **kw2).then_inc(c.h, 1))
        return ctr.v

    def MM(out, lhsT, rhs, start, stop):
        sPE.v += 1
        _SCHED_META["tensor"].append(("i", "sPE", 1))
        E("tensor", lambda e, o=out, l=lhsT, r=rhs, st=start, sp=stop, c=sPE:
          e.matmul(o, l, r, start=st, stop=sp).then_inc(c.h, 1))
        return sPE.v

    def compute_barrier():
        for eng in ("tensor", "vector", "scalar"):
            for c in (sPE, sDVE, sACT):
                W(eng, c, c.v)

    DMA("sync", sK, onec[:, :], ones_col)
    DMA("sync", sK, oner[:, :], ones_row)
    DMA("sync", sK, prot[:, :], prot_c)
    DMA("sync", sK, konst[:, :], konst_c)
    k_all = DMA("sync", sK, idx[:, :], bt_in.rearrange("(b p) -> p b", p=128),
                allow_slow_non_contiguous=True)

    DMA("gpsimd", sGX0, x16[0:TG, :], x_in[0:TG, :])
    x0_cast = sGX0.v
    for i in range(4):
        DMA("gpsimd", sGW, wq16[i * 1024:(i + 1) * 1024, :],
            wq_in[i * 1024:(i + 1) * 1024, :])
    wq_cast_all = sGW.v
    for i in range(1, G):
        DMA("gpsimd", sGX, x16[i * TG:(i + 1) * TG, :], x_in[i * TG:(i + 1) * TG, :])
    x_cast_all = sGX.v

    kc_flat = kc_in.rearrange("b h r d -> b (h r d)")
    vc_flat = vc_in.rearrange("b h r d -> b (h r d)")
    slab_g3 = slab[:, :].rearrange("p (h rd) -> p h rd", rd=BS * HD)
    gath_done = 0
    for (src, dst) in ((kc_flat, kg), (vc_flat, vgo)):
        for b in range(2):
            W("gpsimd", sK, k_all)
            W("gpsimd", sGSt, gath_done)
            sGG.v += 16
            _SCHED_META["gpsimd"].append(("i", "sGG", 16))
            E("gpsimd", lambda e, s=src, bb=b: e.indirect_dma_start(
                slab[:, :], None, s,
                IndirectOffsetOnAxis(idx[:, bb:bb + 1], 0)).then_inc(sGG.h, 16))
            W("sync", sGG, sGG.v)
            dst4 = dst[:, b * 128:(b + 1) * 128, :, :].rearrange("h i r d -> i h (r d)")
            gath_done = DMA("sync", sGSt, dst4, slab_g3)
    gath_all = gath_done
    W("sync", sGSt, gath_all)
    for hk in range(NKV):
        DMA("sync", sVR, vg[hk, :, :, :],
            vgo[hk, :, :, :].rearrange("(c a) r d -> (a r) c d", a=8))
    vr_all = sVR.v

    W("gpsimd", sGW, wq_cast_all)
    for i in range(4):
        DMA("gpsimd", sGW, wo16[i * 1024:(i + 1) * 1024, :],
            wo_in[i * 1024:(i + 1) * 1024, :])
    wo_cast_all = sGW.v

    wb_t = [0, 0, 0, 0]
    wb_s = [0, 0, 0, 0]

    def build_wT(w16_ap, wTs_ap, obw, cast_val):
        stiles = [slab[:, s * 4096:(s + 1) * 4096] for s in range(4)]
        nob = 4096 // obw

        def emit_store(dcx):
            s = dcx % 4
            W("sync", sWB[s], wb_t[s])
            dst = wTs_ap[:, :, dcx, :].rearrange("ob p o -> p ob o")
            src = stiles[s].rearrange("p (ob o) -> p ob o", o=obw)
            wb_s[s] = DMA("sync", sQ, dst, src)

        W("sync", sGW, cast_val)
        for dc in range(32):
            s = dc % 4
            if wb_s[s]:
                W("sync", sQ, sQ.v)
            wb_t[s] = DMA("sync", sWB[s], stiles[s],
                          w16_ap[:, dc * 128:(dc + 1) * 128], transpose=True)
            if dc >= 1:
                emit_store(dc - 1)
        emit_store(31)
        return max(wb_s)

    W("sync", sGSt, gath_all)
    wT_built = build_wT(wq16, wqTs, 128, wq_cast_all)

    ebuf_free_pe = [0, 0, 0]
    ebuf_free_acc = [(None, 0), (None, 0), (None, 0)]
    qpsum_free = [0, 0]
    rpsum_free = [0, 0]
    qraw_free_pe = [0, 0]
    qraw_free_dve = [0, 0]
    spsum_free = [0, 0]
    kv_free = [0, 0]
    slab_free = [0, 0]
    qTar_ready = {}
    ctx_ready = {}
    p3_psum_free = {}
    ctx_evac_done = 0
    tail_free_dve = 0
    tail_free_act = 0
    acc_guard = [(None, 0), (None, 0), (None, 0)]
    pe_sums_prev = 0
    last_zq = 0
    last_zg = 0

    def phase_p1(g):
        W("sync", sPE, sPE.v)
        W("sync", sDVE, sDVE.v)
        W("vector", sGZ, last_zg)
        W("vector", sQ, last_zq)
        DMA("sync", sCS, cosg[:, :], cos_in[:, g * TG:(g + 1) * TG])
        cs_all = DMA("sync", sCS, sing[:, :], sin_in[:, g * TG:(g + 1) * TG])
        if g == 0:
            W("sync", sGX0, x0_cast)
        else:
            W("sync", sGX, x_cast_all)
        xt_all = 0
        for dc in range(32):
            xt_all = DMA("sync", sXT, xcar[:, dc * TG:(dc + 1) * TG],
                         x16[g * TG:(g + 1) * TG, dc * 128:(dc + 1) * 128],
                         transpose=True)
        for jblk in range(NH):
            par = jblk % 2
            W("sync", sQ, wT_built)
            W("sync", sPE, slab_free[par])
            js = slab[:, par * 4096:(par + 1) * 4096]
            c_js = DMA("sync", sJS[par], js.rearrange("p (dc j) -> p dc j", j=128),
                       wqTs[jblk, :, :, :])
            qp = PA[:, par * TG:(par + 1) * TG]
            W("tensor", sJS[par], c_js)
            W("tensor", sXT, xt_all)
            W("tensor", sACT, qpsum_free[par])
            for dc in range(32):
                MM(qp, js[:, dc * 128:(dc + 1) * 128], xcar[:, dc * TG:(dc + 1) * TG],
                   start=(dc == 0), stop=(dc == 31))
            slab_free[par] = sPE.v
            pe_q = sPE.v
            W("scalar", sPE, pe_q)
            W("scalar", sPE, qraw_free_pe[par])
            W("scalar", sDVE, qraw_free_dve[par])
            W("scalar", sACT, qpsum_free[par])
            qpsum_free[par] = OP("scalar", sACT, "copy",
                                 qraw[:, par * TG:(par + 1) * TG], qp)
            rp = PA[:, 2048 + par * TG: 2048 + (par + 1) * TG]
            W("tensor", sACT, qpsum_free[par])
            W("tensor", sDVE, rpsum_free[par])
            pe_rot = MM(rp, prot[:, :], qraw[:, par * TG:(par + 1) * TG],
                        start=True, stop=True)
            qraw_free_pe[par] = pe_rot
            W("vector", sCS, cs_all)
            W("vector", sPE, pe_rot)
            W("vector", sACT, qpsum_free[par])
            t1 = z[:, 1024 + par * TG:1024 + (par + 1) * TG]
            OP("vector", sDVE, "tensor_mul", t1, rp, sing[:, :])
            rpsum_free[par] = sDVE.v
            OP("vector", sDVE, "tensor_mul", z[:, par * TG:(par + 1) * TG],
               qraw[:, par * TG:(par + 1) * TG], cosg[:, :])
            qraw_free_dve[par] = sDVE.v
            W("vector", sDVE, sDVE.v)
            qTar_ready[jblk] = OP("vector", sDVE, "tensor_add",
                                  qTar[:, jblk * TG:(jblk + 1) * TG],
                                  z[:, par * TG:(par + 1) * TG], t1)

    def phase_p2(g):
        nonlocal ctx_evac_done, tail_free_dve, tail_free_act, pe_sums_prev
        W("scalar", sGZ, last_zg)
        W("scalar", sQ, last_zq)
        for hk in range(NKV):
            par = hk % 2
            W("sync", sGSt, gath_all)
            W("sync", sVR, vr_all)
            W("sync", sPE, kv_free[par])
            base = par * 8192
            DMA("sync", sKV[par], kvw[:, base:base + 4096],
                kg[hk, :, :, :].rearrange("i r d -> (i r) d"), transpose=True)
            kv_ready = DMA(
                "sync", sKV[par],
                kvw[:, base + 4096:base + 8192],
                vg[hk, :, :, :].rearrange("p c d -> p (c d)"))
            for pr in range(2):
                h0 = hk * 4 + 2 * pr
                h1 = h0 + 1
                W("tensor", sKV[par], kv_ready)
                W("tensor", sDVE, max(qTar_ready.get(h0, 0), qTar_ready.get(h1, 0)))
                W("vector", sPE, pe_sums_prev)
                W("gpsimd", sPE, pe_sums_prev)
                acc_last = [0, 0, 0]
                acc_init = [False, False, False]

                def emit_scores(j):
                    sp = j % 2
                    scp = PA[:, sp * 1024:(sp + 1) * 1024]
                    kchunk = kvw[:, base + j * 128:base + (j + 1) * 128]
                    W("tensor", sACT, spsum_free[sp])
                    MM(scp[:, 0:TG], kchunk, qTar[:, h0 * TG:(h0 + 1) * TG],
                       start=True, stop=True)
                    return MM(scp[:, TG:2 * TG], kchunk,
                              qTar[:, h1 * TG:(h1 + 1) * TG], start=True, stop=True)

                pe_sc = {0: emit_scores(0)}
                for j in range(32):
                    sp = j % 2
                    eslot = j % 3
                    if j + 1 < 32:
                        pe_sc[j + 1] = emit_scores(j + 1)
                    scp = PA[:, sp * 1024:(sp + 1) * 1024]
                    eb = ebuf[:, eslot * 1024:(eslot + 1) * 1024]
                    W("scalar", sPE, max(pe_sc[j], ebuf_free_pe[eslot]))
                    en, ev = ebuf_free_acc[eslot]
                    if en is not None:
                        W("scalar", en, ev)
                    W("scalar", sK, k_all)
                    act_e = OP("scalar", sACT, "activation", eb, scp, AF.Exp,
                               konst[:, 0:1], SCALE)
                    spsum_free[sp] = act_e
                    if j % 4 == 3:
                        aeng, actr, ab, abuf = "gpsimd", sPL, 2, accg[:, :]
                    elif j % 4 == 1:
                        aeng, actr, ab, abuf = "vector", sDVE, 1, acc[:, 1024:2048]
                    else:
                        aeng, actr, ab, abuf = "vector", sDVE, 0, acc[:, 0:1024]
                    W(aeng, sACT, act_e)
                    gct, gvv = acc_guard[ab]
                    if gct is not None:
                        W(aeng, gct, gvv)
                    if not acc_init[ab]:
                        acc_init[ab] = True
                        acc_last[ab] = OP(aeng, actr, "tensor_copy", abuf, eb)
                    else:
                        acc_last[ab] = OP(aeng, actr, "tensor_add", abuf, abuf, eb)
                    acc_guard[ab] = (actr, acc_last[ab])
                    ebuf_free_acc[eslot] = (actr, acc_last[ab])
                    vchunk = kvw[:, base + 4096 + j * 128:base + 4096 + (j + 1) * 128]
                    W("tensor", sACT, act_e)
                    if j == 0:
                        W("tensor", sDVE, ctx_evac_done)
                    MM(PA[:, 2048:2048 + TG], vchunk, eb[:, 0:TG],
                       start=(j == 0), stop=(j == 31))
                    pe_ctx = MM(PA[:, 2048 + TG:2048 + 2 * TG], vchunk,
                                eb[:, TG:2 * TG], start=(j == 0), stop=(j == 31))
                    ebuf_free_pe[eslot] = pe_ctx
                pe_ctx_last = sPE.v
                W("tensor", sDVE, max(acc_last[0], acc_last[1]))
                W("tensor", sPL, acc_last[2])
                W("tensor", sDVE, tail_free_dve)
                W("tensor", sACT, tail_free_act)
                for half in range(2):
                    tail = PA[0:1, 3072 + half * 512:3072 + (half + 1) * 512]
                    MM(tail, onec[:, :], acc[:, half * 512:(half + 1) * 512],
                       start=True, stop=False)
                    MM(tail, onec[:, :],
                       acc[:, 1024 + half * 512:1024 + (half + 1) * 512],
                       start=False, stop=False)
                    MM(tail, onec[:, :], accg[:, half * 512:(half + 1) * 512],
                       start=False, stop=True)
                pe_sums = sPE.v
                pe_sums_prev = pe_sums
                W("vector", sPE, pe_sums)
                dve_recip = OP("vector", sDVE, "reciprocal", recip[:, :],
                               PA[0:1, 3072:4096])
                W("tensor", sDVE, dve_recip)
                for half in range(2):
                    MM(PA[:, 3072 + half * 512:3072 + (half + 1) * 512], oner[:, :],
                       recip[0:1, half * 512:(half + 1) * 512], start=True, stop=True)
                pe_bc = sPE.v
                W("scalar", sPE, pe_bc)
                tail_free_act = OP("scalar", sACT, "copy", z[:, 0:1024],
                                   PA[:, 3072:4096])
                tail_free_dve = dve_recip
                W("vector", sPE, pe_ctx_last)
                W("vector", sACT, tail_free_act)
                OP("vector", sDVE, "tensor_mul", xcar[:, h0 * TG:(h0 + 1) * TG],
                   PA[:, 2048:2048 + TG], z[:, 0:TG])
                ctx_ready[h0] = sDVE.v
                OP("vector", sDVE, "tensor_mul", xcar[:, h1 * TG:(h1 + 1) * TG],
                   PA[:, 2048 + TG:2048 + 2 * TG], z[:, TG:2 * TG])
                ctx_ready[h1] = sDVE.v
                ctx_evac_done = sDVE.v
            kv_free[par] = sPE.v

    def phase_p3(g):
        for oblk in range(16):
            par = oblk % 2
            W("sync", sPE, slab_free[par])
            W("sync", sQ, wT_built)
            cs = slab[:, par * 8192:(par + 1) * 8192]
            c_cs = DMA("sync", sCOL[par], cs.rearrange("p (c o) -> p c o", o=256),
                       woTs[oblk, :, :, :])
            for rt in range(NRT):
                tile_i = oblk * NRT + rt
                slot = tile_i % 8
                pt = PA[:, slot * 512:slot * 512 + 256]
                W("tensor", sCOL[par], c_cs)
                W("tensor", sDVE, p3_psum_free.get(slot, 0))
                for cblk in range(32):
                    W("tensor", sDVE, ctx_ready.get(cblk, 0))
                    MM(pt, xcar[:, cblk * TG + rt * 128:cblk * TG + (rt + 1) * 128],
                       cs[:, cblk * 256:(cblk + 1) * 256],
                       start=(cblk == 0), stop=(cblk == 31))
                pe_mm = sPE.v
                W("vector", sPE, pe_mm)
                p3_psum_free[slot] = OP(
                    "vector", sDVE, "tensor_copy",
                    attnb[:, rt * D + oblk * 256:rt * D + (oblk + 1) * 256], pt)
            slab_free[par] = sPE.v

    def ln_finish(junk_bf):
        W("vector", sDVE, sDVE.v)
        OP("vector", sDVE, "tensor_scalar_mul", mean[:, :], mean[:, :], 1.0 / D)
        W("vector", sDVE, sDVE.v)
        OP("vector", sDVE, "tensor_scalar", z[:, :], z[:, :], mean[:, 0:1], None,
           op0=ALU.subtract)
        W("vector", sDVE, sDVE.v)
        OP("vector", sDVE, "scalar_tensor_tensor", junk_bf, z[:, :], 1.0, z[:, :],
           ALU.mult, ALU.mult, accum_out=varv[:, :])
        dve_var = sDVE.v
        W("scalar", sDVE, dve_var)
        W("scalar", sK, k_all)
        act_std = OP("scalar", sACT, "activation", stdv[:, :], varv[:, :], AF.Sqrt,
                     konst[:, 1:2], 1.0 / D)
        W("vector", sACT, act_std)
        OP("vector", sDVE, "reciprocal", rstd[:, :], stdv[:, :])
        W("vector", sDVE, sDVE.v)
        OP("vector", sDVE, "tensor_scalar", z[:, :], z[:, :], rstd[:, 0:1], None,
           op0=ALU.mult)
        return sDVE.v

    def phase_p3b(g):
        nonlocal last_zq, last_zg
        for rt in range(NRT):
            rows = slice(g * TG + rt * 128, g * TG + (rt + 1) * 128)
            W("sync", sQ, last_zq)
            W("sync", sGZ, last_zg)
            W("sync", sDVE, sDVE.v)
            c_x = DMA("sync", sZX, z[:, :], x_in[rows, :])
            W("vector", sZX, c_x)
            W("vector", sDVE, sDVE.v)
            OP("vector", sDVE, "scalar_tensor_tensor", z[:, :], z[:, :], 1.0,
               attnb[:, rt * D:(rt + 1) * D], ALU.mult, ALU.add,
               accum_out=mean[:, :])
            dve_x1 = ln_finish(attnb[:, rt * D:(rt + 1) * D])
            W("sync", sDVE, dve_x1)
            last_zq = DMA("sync", sQ, x1f[rows, :], z[:, :])
            W("gpsimd", sDVE, dve_x1)
            last_zg = DMA("gpsimd", sGZ, x1b[rows, :], z[:, :])

    def phase_p4():
        nonlocal last_zq, last_zg
        p4_free = [0, 0]
        for c in range(4):
            DMA("gpsimd", sGF, kvw[:, c * 4096:(c + 1) * 4096],
                fk_in[c * 128:(c + 1) * 128, :])
        w_all = sGF.v
        x1b_all = last_zg
        for g in range(G):
            W("sync", sGZ, x1b_all)
            W("sync", sQ, last_zq)
            W("sync", sPE, sPE.v)
            xt_all = 0
            for ic in range(32):
                xt_all = DMA("sync", sXT, xcar[:, ic * TG:(ic + 1) * TG],
                             x1b[g * TG:(g + 1) * TG, ic * 128:(ic + 1) * 128],
                             transpose=True)
            for rt in range(NRT):
                rows = slice(g * TG + rt * 128, g * TG + (rt + 1) * 128)
                W("sync", sQ, last_zq)
                W("sync", sDVE, sDVE.v)
                c_x1 = DMA("sync", sZX, z[:, :], x1f[rows, :])
                accs = (mean, var2)
                for half in range(2):
                    pbase = half * 2048
                    ph = PA[:, pbase:pbase + 2048]
                    W("tensor", sXT, xt_all)
                    W("tensor", sGF, w_all)
                    W("tensor", sDVE, p4_free[half])
                    for ic in range(32):
                        for nn2 in range(4):
                            MM(ph[:, nn2 * 512:(nn2 + 1) * 512],
                               xcar[:, ic * TG + rt * 128:ic * TG + (rt + 1) * 128],
                               kvw[:, (ic % 4) * 4096 + pbase + nn2 * 512:
                                   (ic % 4) * 4096 + pbase + (nn2 + 1) * 512],
                               start=(ic == 0), stop=(ic == 31))
                    pe_h = sPE.v
                    W("vector", sPE, pe_h)
                    W("vector", sZX, c_x1)
                    OP("vector", sDVE, "scalar_tensor_tensor",
                       z[:, pbase:pbase + 2048], z[:, pbase:pbase + 2048], 1.0, ph,
                       ALU.mult, ALU.add, accum_out=accs[half][:, :])
                    p4_free[half] = sDVE.v
                W("vector", sDVE, sDVE.v)
                OP("vector", sDVE, "tensor_add", mean[:, :], mean[:, :], var2[:, :])
                dve_y = ln_finish(attnb[:, 0:D])
                W("sync", sDVE, dve_y)
                last_zq = DMA("sync", sQ, y_out[rows, :], z[:, :])

    woT_built = False
    for g in range(G):
        compute_barrier()
        phase_p1(g)
        compute_barrier()
        phase_p2(g)
        if not woT_built:
            W("sync", sPE, sPE.v)
            wT_built = build_wT(wo16, woTs, 256, wo_cast_all)
            woT_built = True
        compute_barrier()
        phase_p3(g)
        compute_barrier()
        phase_p3b(g)
    compute_barrier()
    phase_p4()
    for c in dma_ctrs:
        W("sync", c, c.v)

    import contextlib
    es = contextlib.ExitStack()
    for n in names:
        C[n].h = es.enter_context(nc.semaphore(n))
    with nc.Block() as block:
        @block.sync
        def _(e):
            for f in prog["sync"]:
                f(e)

        @block.gpsimd
        def _(e):
            for f in prog["gpsimd"]:
                f(e)

        @block.tensor
        def _(e):
            for f in prog["tensor"]:
                f(e)

        @block.vector
        def _(e):
            for f in prog["vector"]:
                f(e)

        @block.scalar
        def _(e):
            for f in prog["scalar"]:
                f(e)
    es.close()
    n_inst = {k: len(v) for k, v in prog.items()}
    print("instructions per engine:", n_inst, "total:", sum(n_inst.values()))
    return nc


def _get_nc():
    global _NC_CACHE
    if _NC_CACHE is None:
        _NC_CACHE = build_nc()
    return _NC_CACHE


def kernel(L0_attn_hidden_states, L0_attn_k_cache, L0_attn_v_cache,
           L0_attn_Wq, L0_attn_Wo, L0_attn_block_table, L0_ffn_kernel):
    x = np.ascontiguousarray(np.asarray(L0_attn_hidden_states, dtype=np.float32)
                             .reshape(16384, D))
    wq = np.ascontiguousarray(np.asarray(L0_attn_Wq, dtype=np.float32))
    wo = np.ascontiguousarray(np.asarray(L0_attn_Wo, dtype=np.float32))
    kc = np.ascontiguousarray(np.asarray(L0_attn_k_cache, dtype=np.float32))
    vc = np.ascontiguousarray(np.asarray(L0_attn_v_cache, dtype=np.float32))
    bt = np.ascontiguousarray(np.asarray(L0_attn_block_table, dtype=np.int32))
    fk = np.ascontiguousarray(np.asarray(L0_ffn_kernel, dtype=np.int8).reshape(512, D))

    nc = _get_nc()
    in_maps = []
    for c in range(NCORES):
        cosT, sinT = _host_tables(c)
        in_maps.append({
            "x_in": np.ascontiguousarray(x[c * T:(c + 1) * T]),
            "wq_in": wq, "wo_in": wo, "kc_in": kc, "vc_in": vc,
            "bt_in": bt, "fk_in": fk,
            "cos_in": cosT, "sin_in": sinT,
        })
    res = run_bass_kernel_spmd(nc, in_maps, core_ids=list(range(NCORES)))
    out = np.concatenate([np.asarray(r["y_out"]) for r in res.results], axis=0)
    return out.reshape(4, 4096, 4096)


if __name__ == "__main__":
    build_nc()
    print("build ok")


# version 06
# speedup vs baseline: 1.6609x; 1.0154x over previous
import numpy as np
import ml_dtypes
import concourse.bass as bass
import concourse.mybir as mybir
from concourse.bass import IndirectOffsetOnAxis
from concourse.bass_utils import run_bass_kernel_spmd

F32 = mybir.dt.float32
BF16 = mybir.dt.bfloat16
I32 = mybir.dt.int32
I8 = mybir.dt.int8
AF = mybir.ActivationFunctionType
ALU = mybir.AluOpType

NCORES = 8
T = 2048
D = 4096
G = 4
TG = T // G
NRT = TG // 128
NH = 32
HD = 128
NKV = 8
NBLK, BS, NSEL = 512, 16, 256
SCALE = float(np.float32(1.0) / np.float32(np.sqrt(np.float32(128.0))))
CSHIFT = 10.0
EPS = 1e-5
SEQ = 4096


def _host_tables(core):
    inv_freq = (np.float32(1.0) /
                np.power(np.float32(10000.0),
                         np.arange(0, HD, 2, dtype=np.float32) / np.float32(HD)))
    t = np.arange(SEQ, dtype=np.float32)
    freqs = np.outer(t, inv_freq).astype(np.float32)
    emb = np.concatenate([freqs, freqs], -1)
    cos = np.cos(emb).astype(np.float32)
    sin = np.sin(emb).astype(np.float32)
    off = (core % 2) * T
    return (np.ascontiguousarray(cos[off:off + T].T).astype(ml_dtypes.bfloat16),
            np.ascontiguousarray(sin[off:off + T].T).astype(ml_dtypes.bfloat16))


def _prot():
    p = np.zeros((128, 128), dtype=np.float32)
    for s in range(64):
        p[s, s + 64] = 1.0
    for s in range(64, 128):
        p[s, s - 64] = -1.0
    return p.astype(ml_dtypes.bfloat16)


class _Ctr:
    __slots__ = ("h", "v", "name")

    def __init__(self, name):
        self.h = None
        self.v = 0
        self.name = name


_SCHED_META = None
_NC_CACHE = None


def build_nc():
    global _SCHED_META
    nc = bass.Bass("TRN2", target_bir_lowering=False)

    x_in = nc.dram_tensor("x_in", [T, D], F32, kind="ExternalInput").ap()
    wq_in = nc.dram_tensor("wq_in", [D, D], F32, kind="ExternalInput").ap()
    wo_in = nc.dram_tensor("wo_in", [D, D], F32, kind="ExternalInput").ap()
    kc_in = nc.dram_tensor("kc_in", [NBLK, NKV, BS, HD], F32, kind="ExternalInput").ap()
    vc_in = nc.dram_tensor("vc_in", [NBLK, NKV, BS, HD], F32, kind="ExternalInput").ap()
    bt_in = nc.dram_tensor("bt_in", [NSEL], I32, kind="ExternalInput").ap()
    fk_in = nc.dram_tensor("fk_in", [512, D], I8, kind="ExternalInput").ap()
    cos_in = nc.dram_tensor("cos_in", [HD, T], BF16, kind="ExternalInput").ap()
    sin_in = nc.dram_tensor("sin_in", [HD, T], BF16, kind="ExternalInput").ap()
    y_out = nc.dram_tensor("y_out", [T, D], F32, kind="ExternalOutput").ap()

    x16 = nc.dram_tensor("x16_s", [T, D], BF16, kind="Internal").ap()
    wq16 = nc.dram_tensor("wq16_s", [D, D], BF16, kind="Internal").ap()
    wo16 = nc.dram_tensor("wo16_s", [D, D], BF16, kind="Internal").ap()
    wqTs = nc.dram_tensor("wqT_s", [32, 128, 32, 128], BF16, kind="Internal").ap()
    woTs = nc.dram_tensor("woT_s", [16, 128, 32, 256], BF16, kind="Internal").ap()
    kg = nc.dram_tensor("kg_s", [NKV, NSEL, BS, HD], BF16, kind="Internal").ap()
    vgo = nc.dram_tensor("vgo_s", [NKV, NSEL, BS, HD], BF16, kind="Internal").ap()
    vg = nc.dram_tensor("vg_s", [NKV, 128, 32, HD], BF16, kind="Internal").ap()
    x1f = nc.dram_tensor("x1f_s", [T, D], F32, kind="Internal").ap()
    x1b = nc.dram_tensor("x1b_s", [T, D], BF16, kind="Internal").ap()

    ones_col = nc.inline_tensor(np.ones((128, 1), dtype=np.float32), name="ones_col").ap()
    ones_row = nc.inline_tensor(np.ones((1, 128), dtype=np.float32), name="ones_row").ap()
    prot_c = nc.inline_tensor(_prot(), name="prot_c").ap()
    konst_c = nc.inline_tensor(
        np.tile(np.array([[-CSHIFT, EPS, 0.0, 0.0]], dtype=np.float32), (128, 1)),
        name="konst_c").ap()

    sbt = nc.alloc_sbuf_tensor
    qTar = sbt("qTar", [128, NH * TG], BF16)
    xcar = sbt("xcar", [128, 32 * TG], BF16)
    kvw = sbt("kvw", [128, 16384], BF16)
    slab = sbt("slab", [128, 16384], BF16)
    ebuf = sbt("ebuf", [128, 4 * 1024], BF16)
    acc = sbt("acc", [128, 2 * 1024], F32)
    accg = sbt("accg", [128, 1024], F32)
    attnb = sbt("attnb", [128, NRT * D], BF16)
    z = sbt("z", [128, D], F32)
    qraw = sbt("qraw", [128, 2 * TG], BF16)
    cosg = sbt("cosg", [128, TG], BF16)
    sing = sbt("sing", [128, TG], BF16)
    idx = sbt("idx", [128, 2], I32)
    onec = sbt("onec", [128, 1], F32)
    oner = sbt("oner", [1, 128], F32)
    prot = sbt("prot", [128, 128], BF16)
    konst = sbt("konst", [128, 4], F32)
    recip = sbt("recip", [1, 1024], F32)
    mean = sbt("mean", [128, 1], F32)
    var2 = sbt("var2", [128, 1], F32)
    varv = sbt("varv", [128, 1], F32)
    stdv = sbt("stdv", [128, 1], F32)
    rstd = sbt("rstd", [128, 1], F32)

    PA = nc.alloc_psum_tensor("PA", [128, 4096], F32)

    prog = {k: [] for k in ("sync", "gpsimd", "tensor", "vector", "scalar")}
    waited = {k: {} for k in prog}
    _SCHED_META = {k: [] for k in prog}

    names = ("sQ", "sK", "sGSt", "sXT", "sCS", "sJS0", "sJS1", "sKV0", "sKV1",
             "sCOL0", "sCOL1", "sZX", "sGX", "sGX0", "sGG", "sGW", "sGZ", "sGF",
             "sVR", "sWB0", "sWB1", "sWB2", "sWB3",
             "sPE", "sDVE", "sACT", "sPL")
    C = {n: _Ctr(n) for n in names}
    sQ, sK, sGSt, sXT, sCS = C["sQ"], C["sK"], C["sGSt"], C["sXT"], C["sCS"]
    sJS = [C["sJS0"], C["sJS1"]]
    sKV = [C["sKV0"], C["sKV1"]]
    sCOL = [C["sCOL0"], C["sCOL1"]]
    sZX, sGX, sGX0, sGG, sGW, sGZ, sGF = (C["sZX"], C["sGX"], C["sGX0"],
                                          C["sGG"], C["sGW"], C["sGZ"], C["sGF"])
    sPE, sDVE, sACT, sPL = C["sPE"], C["sDVE"], C["sACT"], C["sPL"]
    sWB = [C["sWB0"], C["sWB1"], C["sWB2"], C["sWB3"]]
    sVR = C["sVR"]
    dma_ctrs = [sQ, sK, sGSt, sXT, sCS, sJS[0], sJS[1], sKV[0], sKV[1],
                sCOL[0], sCOL[1], sZX, sGX, sGX0, sGG, sGW, sGZ, sGF, sVR] + sWB

    def E(eng, fn):
        prog[eng].append(fn)

    def W(eng, ctr, val):
        if val <= 0:
            return
        if val <= waited[eng].get(ctr.name, 0):
            return
        waited[eng][ctr.name] = val
        _SCHED_META[eng].append(("w", ctr.name, val))
        E(eng, lambda e, c=ctr, v=val: e.wait_ge(c.h, v))

    def DMA(qname, ctr, out, in_, **kw):
        ctr.v += 16
        _SCHED_META[qname].append(("i", ctr.name, 16))
        E(qname, lambda e, o=out, i=in_, c=ctr, kw2=dict(kw):
          e.dma_start(out=o, in_=i, **kw2).then_inc(c.h, 16))
        return ctr.v

    def OP(eng, ctr, meth, *args, **kw):
        ctr.v += 1
        _SCHED_META[eng].append(("i", ctr.name, 1))
        E(eng, lambda e, m=meth, a=args, c=ctr, kw2=dict(kw):
          getattr(e, m)(*a, **kw2).then_inc(c.h, 1))
        return ctr.v

    def MM(out, lhsT, rhs, start, stop):
        sPE.v += 1
        _SCHED_META["tensor"].append(("i", "sPE", 1))
        E("tensor", lambda e, o=out, l=lhsT, r=rhs, st=start, sp=stop, c=sPE:
          e.matmul(o, l, r, start=st, stop=sp).then_inc(c.h, 1))
        return sPE.v

    def compute_barrier():
        for eng in ("tensor", "vector", "scalar"):
            for c in (sPE, sDVE, sACT):
                W(eng, c, c.v)

    DMA("sync", sK, onec[:, :], ones_col)
    DMA("sync", sK, oner[:, :], ones_row)
    DMA("sync", sK, prot[:, :], prot_c)
    DMA("sync", sK, konst[:, :], konst_c)
    k_all = DMA("sync", sK, idx[:, :], bt_in.rearrange("(b p) -> p b", p=128),
                allow_slow_non_contiguous=True)

    DMA("gpsimd", sGX0, x16[0:TG, :], x_in[0:TG, :])
    x0_cast = sGX0.v
    for i in range(4):
        DMA("gpsimd", sGW, wq16[i * 1024:(i + 1) * 1024, :],
            wq_in[i * 1024:(i + 1) * 1024, :])
    wq_cast_all = sGW.v
    for i in range(1, G):
        DMA("gpsimd", sGX, x16[i * TG:(i + 1) * TG, :], x_in[i * TG:(i + 1) * TG, :])
    x_cast_all = sGX.v

    kc_flat = kc_in.rearrange("b h r d -> b (h r d)")
    vc_flat = vc_in.rearrange("b h r d -> b (h r d)")
    slab_g3 = slab[:, :].rearrange("p (h rd) -> p h rd", rd=BS * HD)
    gath_done = 0
    for (src, dst) in ((kc_flat, kg), (vc_flat, vgo)):
        for b in range(2):
            W("gpsimd", sK, k_all)
            W("gpsimd", sGSt, gath_done)
            sGG.v += 16
            _SCHED_META["gpsimd"].append(("i", "sGG", 16))
            E("gpsimd", lambda e, s=src, bb=b: e.indirect_dma_start(
                slab[:, :], None, s,
                IndirectOffsetOnAxis(idx[:, bb:bb + 1], 0)).then_inc(sGG.h, 16))
            W("sync", sGG, sGG.v)
            dst4 = dst[:, b * 128:(b + 1) * 128, :, :].rearrange("h i r d -> i h (r d)")
            gath_done = DMA("sync", sGSt, dst4, slab_g3)
    gath_all = gath_done
    W("sync", sGSt, gath_all)
    for hk in range(NKV):
        DMA("sync", sVR, vg[hk, :, :, :],
            vgo[hk, :, :, :].rearrange("(c a) r d -> (a r) c d", a=8))
    vr_all = sVR.v

    W("gpsimd", sGW, wq_cast_all)
    for i in range(4):
        DMA("gpsimd", sGW, wo16[i * 1024:(i + 1) * 1024, :],
            wo_in[i * 1024:(i + 1) * 1024, :])
    wo_cast_all = sGW.v

    wb_t = [0, 0, 0, 0]
    wb_s = [0, 0, 0, 0]

    def build_wT(w16_ap, wTs_ap, obw, cast_val):
        stiles = [slab[:, s * 4096:(s + 1) * 4096] for s in range(4)]
        nob = 4096 // obw

        def emit_store(dcx):
            s = dcx % 4
            W("sync", sWB[s], wb_t[s])
            dst = wTs_ap[:, :, dcx, :].rearrange("ob p o -> p ob o")
            src = stiles[s].rearrange("p (ob o) -> p ob o", o=obw)
            wb_s[s] = DMA("sync", sQ, dst, src)

        W("sync", sGW, cast_val)
        for dc in range(32):
            s = dc % 4
            if wb_s[s]:
                W("sync", sQ, sQ.v)
            wb_t[s] = DMA("sync", sWB[s], stiles[s],
                          w16_ap[:, dc * 128:(dc + 1) * 128], transpose=True)
            if dc >= 1:
                emit_store(dc - 1)
        emit_store(31)
        return max(wb_s)

    W("sync", sGSt, gath_all)
    wT_built = build_wT(wq16, wqTs, 128, wq_cast_all)

    ebuf_free_pe = [0, 0, 0, 0]
    ebuf_free_acc = [(None, 0)] * 4
    qpsum_free = [0, 0]
    rpsum_free = [0, 0]
    qraw_free_pe = [0, 0]
    qraw_free_dve = [0, 0]
    spsum_free = [0, 0]
    kv_free = [0, 0]
    slab_free = [0, 0]
    qTar_ready = {}
    ctx_ready = {}
    p3_psum_free = {}
    ctx_evac_done = 0
    tail_free_dve = 0
    tail_free_act = 0
    acc_guard = [(None, 0), (None, 0), (None, 0)]
    pe_sums_prev = 0
    last_zq = 0
    last_zg = 0

    def phase_p1(g):
        W("sync", sPE, sPE.v)
        W("sync", sDVE, sDVE.v)
        W("vector", sGZ, last_zg)
        W("vector", sQ, last_zq)
        DMA("sync", sCS, cosg[:, :], cos_in[:, g * TG:(g + 1) * TG])
        cs_all = DMA("sync", sCS, sing[:, :], sin_in[:, g * TG:(g + 1) * TG])
        if g == 0:
            W("sync", sGX0, x0_cast)
        else:
            W("sync", sGX, x_cast_all)
        xt_all = 0
        for dc in range(32):
            xt_all = DMA("sync", sXT, xcar[:, dc * TG:(dc + 1) * TG],
                         x16[g * TG:(g + 1) * TG, dc * 128:(dc + 1) * 128],
                         transpose=True)
        for jblk in range(NH):
            par = jblk % 2
            W("sync", sQ, wT_built)
            W("sync", sPE, slab_free[par])
            js = slab[:, par * 4096:(par + 1) * 4096]
            c_js = DMA("sync", sJS[par], js.rearrange("p (dc j) -> p dc j", j=128),
                       wqTs[jblk, :, :, :])
            qp = PA[:, par * TG:(par + 1) * TG]
            W("tensor", sJS[par], c_js)
            W("tensor", sXT, xt_all)
            W("tensor", sACT, qpsum_free[par])
            for dc in range(32):
                MM(qp, js[:, dc * 128:(dc + 1) * 128], xcar[:, dc * TG:(dc + 1) * TG],
                   start=(dc == 0), stop=(dc == 31))
            slab_free[par] = sPE.v
            pe_q = sPE.v
            W("scalar", sPE, pe_q)
            W("scalar", sPE, qraw_free_pe[par])
            W("scalar", sDVE, qraw_free_dve[par])
            W("scalar", sACT, qpsum_free[par])
            qpsum_free[par] = OP("scalar", sACT, "copy",
                                 qraw[:, par * TG:(par + 1) * TG], qp)
            rp = PA[:, 2048 + par * TG: 2048 + (par + 1) * TG]
            W("tensor", sACT, qpsum_free[par])
            W("tensor", sDVE, rpsum_free[par])
            pe_rot = MM(rp, prot[:, :], qraw[:, par * TG:(par + 1) * TG],
                        start=True, stop=True)
            qraw_free_pe[par] = pe_rot
            W("vector", sCS, cs_all)
            W("vector", sPE, pe_rot)
            W("vector", sACT, qpsum_free[par])
            t1 = z[:, 1024 + par * TG:1024 + (par + 1) * TG]
            OP("vector", sDVE, "tensor_mul", t1, rp, sing[:, :])
            rpsum_free[par] = sDVE.v
            OP("vector", sDVE, "tensor_mul", z[:, par * TG:(par + 1) * TG],
               qraw[:, par * TG:(par + 1) * TG], cosg[:, :])
            qraw_free_dve[par] = sDVE.v
            W("vector", sDVE, sDVE.v)
            qTar_ready[jblk] = OP("vector", sDVE, "tensor_add",
                                  qTar[:, jblk * TG:(jblk + 1) * TG],
                                  z[:, par * TG:(par + 1) * TG], t1)

    def phase_p2(g):
        nonlocal ctx_evac_done, tail_free_dve, tail_free_act, pe_sums_prev
        W("scalar", sGZ, last_zg)
        W("scalar", sQ, last_zq)
        pending = None

        def emit_tail2(st):
            nonlocal tail_free_dve, tail_free_act
            W("vector", sPE, st["pe_sums"])
            dve_recip = OP("vector", sDVE, "reciprocal", recip[:, :],
                           PA[0:1, 3072:4096])
            tail_free_dve = dve_recip
            W("tensor", sDVE, dve_recip)
            for half in range(2):
                MM(PA[:, 3072 + half * 512:3072 + (half + 1) * 512], oner[:, :],
                   recip[0:1, half * 512:(half + 1) * 512], start=True, stop=True)
            pe_bc = sPE.v
            W("scalar", sPE, pe_bc)
            tail_free_act = OP("scalar", sACT, "copy", z[:, 0:1024],
                               PA[:, 3072:4096])
            h0, h1 = st["h0"], st["h1"]
            W("vector", sACT, tail_free_act)
            W("vector", sDVE, st["raw_evac"])
            OP("vector", sDVE, "tensor_mul", xcar[:, h0 * TG:(h0 + 1) * TG],
               xcar[:, h0 * TG:(h0 + 1) * TG], z[:, 0:TG])
            ctx_ready[h0] = sDVE.v
            OP("vector", sDVE, "tensor_mul", xcar[:, h1 * TG:(h1 + 1) * TG],
               xcar[:, h1 * TG:(h1 + 1) * TG], z[:, TG:2 * TG])
            ctx_ready[h1] = sDVE.v

        for hk in range(NKV):
            par = hk % 2
            W("sync", sGSt, gath_all)
            W("sync", sVR, vr_all)
            W("sync", sPE, kv_free[par])
            base = par * 8192
            DMA("sync", sKV[par], kvw[:, base:base + 4096],
                kg[hk, :, :, :].rearrange("i r d -> (i r) d"), transpose=True)
            kv_ready = DMA(
                "sync", sKV[par],
                kvw[:, base + 4096:base + 8192],
                vg[hk, :, :, :].rearrange("p c d -> p (c d)"))
            for pr in range(2):
                h0 = hk * 4 + 2 * pr
                h1 = h0 + 1
                W("tensor", sKV[par], kv_ready)
                W("tensor", sDVE, max(qTar_ready.get(h0, 0), qTar_ready.get(h1, 0)))
                acc_last = [0, 0, 0]
                acc_init = [False, False, False]

                def emit_scores(j):
                    sp = j % 2
                    scp = PA[:, sp * 1024:(sp + 1) * 1024]
                    kchunk = kvw[:, base + j * 128:base + (j + 1) * 128]
                    W("tensor", sACT, spsum_free[sp])
                    MM(scp[:, 0:TG], kchunk, qTar[:, h0 * TG:(h0 + 1) * TG],
                       start=True, stop=True)
                    return MM(scp[:, TG:2 * TG], kchunk,
                              qTar[:, h1 * TG:(h1 + 1) * TG], start=True, stop=True)

                pe_sc = {0: emit_scores(0)}
                for j in range(32):
                    sp = j % 2
                    eslot = j % 4
                    if j + 1 < 32:
                        pe_sc[j + 1] = emit_scores(j + 1)
                    if j == 4 and pending is not None:
                        emit_tail2(pending)
                        pending = None
                    scp = PA[:, sp * 1024:(sp + 1) * 1024]
                    eb = ebuf[:, eslot * 1024:(eslot + 1) * 1024]
                    W("scalar", sPE, max(pe_sc[j], ebuf_free_pe[eslot]))
                    en, ev = ebuf_free_acc[eslot]
                    if en is not None:
                        W("scalar", en, ev)
                    W("scalar", sK, k_all)
                    act_e = OP("scalar", sACT, "activation", eb, scp, AF.Exp,
                               konst[:, 0:1], SCALE)
                    spsum_free[sp] = act_e
                    if j % 4 == 3:
                        aeng, actr, ab, abuf = "gpsimd", sPL, 2, accg[:, :]
                    elif j % 4 == 1:
                        aeng, actr, ab, abuf = "vector", sDVE, 1, acc[:, 1024:2048]
                    else:
                        aeng, actr, ab, abuf = "vector", sDVE, 0, acc[:, 0:1024]
                    W(aeng, sACT, act_e)
                    W(aeng, sPE, pe_sums_prev)
                    gct, gvv = acc_guard[ab]
                    if gct is not None:
                        W(aeng, gct, gvv)
                    if not acc_init[ab]:
                        acc_init[ab] = True
                        acc_last[ab] = OP(aeng, actr, "tensor_copy", abuf, eb)
                    else:
                        acc_last[ab] = OP(aeng, actr, "tensor_add", abuf, abuf, eb)
                    acc_guard[ab] = (actr, acc_last[ab])
                    ebuf_free_acc[eslot] = (actr, acc_last[ab])
                    vchunk = kvw[:, base + 4096 + j * 128:base + 4096 + (j + 1) * 128]
                    W("tensor", sACT, act_e)
                    if j == 0:
                        W("tensor", sDVE, ctx_evac_done)
                    MM(PA[:, 2048:2048 + TG], vchunk, eb[:, 0:TG],
                       start=(j == 0), stop=(j == 31))
                    pe_ctx = MM(PA[:, 2048 + TG:2048 + 2 * TG], vchunk,
                                eb[:, TG:2 * TG], start=(j == 0), stop=(j == 31))
                    ebuf_free_pe[eslot] = pe_ctx
                pe_ctx_last = sPE.v
                W("vector", sPE, pe_ctx_last)
                OP("vector", sDVE, "tensor_copy", xcar[:, h0 * TG:(h0 + 1) * TG],
                   PA[:, 2048:2048 + TG])
                raw_evac = OP("vector", sDVE, "tensor_copy",
                              xcar[:, h1 * TG:(h1 + 1) * TG],
                              PA[:, 2048 + TG:2048 + 2 * TG])
                ctx_evac_done = raw_evac
                W("tensor", sDVE, max(acc_last[0], acc_last[1]))
                W("tensor", sPL, acc_last[2])
                W("tensor", sDVE, tail_free_dve)
                W("tensor", sACT, tail_free_act)
                for half in range(2):
                    tail = PA[0:1, 3072 + half * 512:3072 + (half + 1) * 512]
                    MM(tail, onec[:, :], acc[:, half * 512:(half + 1) * 512],
                       start=True, stop=False)
                    MM(tail, onec[:, :],
                       acc[:, 1024 + half * 512:1024 + (half + 1) * 512],
                       start=False, stop=False)
                    MM(tail, onec[:, :], accg[:, half * 512:(half + 1) * 512],
                       start=False, stop=True)
                pe_sums_prev = sPE.v
                pending = {"pe_sums": pe_sums_prev, "h0": h0, "h1": h1,
                           "raw_evac": raw_evac}
            kv_free[par] = sPE.v
        emit_tail2(pending)
        pending = None

    def phase_p3(g):
        for oblk in range(16):
            par = oblk % 2
            W("sync", sPE, slab_free[par])
            W("sync", sQ, wT_built)
            cs = slab[:, par * 8192:(par + 1) * 8192]
            c_cs = DMA("sync", sCOL[par], cs.rearrange("p (c o) -> p c o", o=256),
                       woTs[oblk, :, :, :])
            for rt in range(NRT):
                tile_i = oblk * NRT + rt
                slot = tile_i % 8
                pt = PA[:, slot * 512:slot * 512 + 256]
                W("tensor", sCOL[par], c_cs)
                W("tensor", sDVE, p3_psum_free.get(slot, 0))
                for cblk in range(32):
                    W("tensor", sDVE, ctx_ready.get(cblk, 0))
                    MM(pt, xcar[:, cblk * TG + rt * 128:cblk * TG + (rt + 1) * 128],
                       cs[:, cblk * 256:(cblk + 1) * 256],
                       start=(cblk == 0), stop=(cblk == 31))
                pe_mm = sPE.v
                W("vector", sPE, pe_mm)
                p3_psum_free[slot] = OP(
                    "vector", sDVE, "tensor_copy",
                    attnb[:, rt * D + oblk * 256:rt * D + (oblk + 1) * 256], pt)
            slab_free[par] = sPE.v

    def ln_finish(junk_bf):
        W("vector", sDVE, sDVE.v)
        OP("vector", sDVE, "tensor_scalar_mul", mean[:, :], mean[:, :], 1.0 / D)
        W("vector", sDVE, sDVE.v)
        OP("vector", sDVE, "tensor_scalar", z[:, :], z[:, :], mean[:, 0:1], None,
           op0=ALU.subtract)
        W("vector", sDVE, sDVE.v)
        OP("vector", sDVE, "scalar_tensor_tensor", junk_bf, z[:, :], 1.0, z[:, :],
           ALU.mult, ALU.mult, accum_out=varv[:, :])
        dve_var = sDVE.v
        W("scalar", sDVE, dve_var)
        W("scalar", sK, k_all)
        act_std = OP("scalar", sACT, "activation", stdv[:, :], varv[:, :], AF.Sqrt,
                     konst[:, 1:2], 1.0 / D)
        W("vector", sACT, act_std)
        OP("vector", sDVE, "reciprocal", rstd[:, :], stdv[:, :])
        W("vector", sDVE, sDVE.v)
        OP("vector", sDVE, "tensor_scalar", z[:, :], z[:, :], rstd[:, 0:1], None,
           op0=ALU.mult)
        return sDVE.v

    def phase_p3b(g):
        nonlocal last_zq, last_zg
        for rt in range(NRT):
            rows = slice(g * TG + rt * 128, g * TG + (rt + 1) * 128)
            W("sync", sQ, last_zq)
            W("sync", sGZ, last_zg)
            W("sync", sDVE, sDVE.v)
            c_x = DMA("sync", sZX, z[:, :], x_in[rows, :])
            W("vector", sZX, c_x)
            W("vector", sDVE, sDVE.v)
            OP("vector", sDVE, "scalar_tensor_tensor", z[:, :], z[:, :], 1.0,
               attnb[:, rt * D:(rt + 1) * D], ALU.mult, ALU.add,
               accum_out=mean[:, :])
            dve_x1 = ln_finish(attnb[:, rt * D:(rt + 1) * D])
            W("sync", sDVE, dve_x1)
            last_zq = DMA("sync", sQ, x1f[rows, :], z[:, :])
            W("gpsimd", sDVE, dve_x1)
            last_zg = DMA("gpsimd", sGZ, x1b[rows, :], z[:, :])

    def phase_p4():
        nonlocal last_zq, last_zg
        p4_free = [0, 0]
        for c in range(4):
            DMA("gpsimd", sGF, kvw[:, c * 4096:(c + 1) * 4096],
                fk_in[c * 128:(c + 1) * 128, :])
        w_all = sGF.v
        x1b_all = last_zg
        for g in range(G):
            W("sync", sGZ, x1b_all)
            W("sync", sQ, last_zq)
            W("sync", sPE, sPE.v)
            xt_all = 0
            for ic in range(32):
                xt_all = DMA("sync", sXT, xcar[:, ic * TG:(ic + 1) * TG],
                             x1b[g * TG:(g + 1) * TG, ic * 128:(ic + 1) * 128],
                             transpose=True)
            for rt in range(NRT):
                rows = slice(g * TG + rt * 128, g * TG + (rt + 1) * 128)
                W("sync", sQ, last_zq)
                W("sync", sDVE, sDVE.v)
                c_x1 = DMA("sync", sZX, z[:, :], x1f[rows, :])
                accs = (mean, var2)
                for half in range(2):
                    pbase = half * 2048
                    ph = PA[:, pbase:pbase + 2048]
                    W("tensor", sXT, xt_all)
                    W("tensor", sGF, w_all)
                    W("tensor", sDVE, p4_free[half])
                    for ic in range(32):
                        for nn2 in range(4):
                            MM(ph[:, nn2 * 512:(nn2 + 1) * 512],
                               xcar[:, ic * TG + rt * 128:ic * TG + (rt + 1) * 128],
                               kvw[:, (ic % 4) * 4096 + pbase + nn2 * 512:
                                   (ic % 4) * 4096 + pbase + (nn2 + 1) * 512],
                               start=(ic == 0), stop=(ic == 31))
                    pe_h = sPE.v
                    W("vector", sPE, pe_h)
                    W("vector", sZX, c_x1)
                    OP("vector", sDVE, "scalar_tensor_tensor",
                       z[:, pbase:pbase + 2048], z[:, pbase:pbase + 2048], 1.0, ph,
                       ALU.mult, ALU.add, accum_out=accs[half][:, :])
                    p4_free[half] = sDVE.v
                W("vector", sDVE, sDVE.v)
                OP("vector", sDVE, "tensor_add", mean[:, :], mean[:, :], var2[:, :])
                dve_y = ln_finish(attnb[:, 0:D])
                W("sync", sDVE, dve_y)
                last_zq = DMA("sync", sQ, y_out[rows, :], z[:, :])

    woT_built = False
    for g in range(G):
        compute_barrier()
        phase_p1(g)
        compute_barrier()
        phase_p2(g)
        if not woT_built:
            W("sync", sPE, sPE.v)
            wT_built = build_wT(wo16, woTs, 256, wo_cast_all)
            woT_built = True
        compute_barrier()
        phase_p3(g)
        compute_barrier()
        phase_p3b(g)
    compute_barrier()
    phase_p4()
    for c in dma_ctrs:
        W("sync", c, c.v)

    import contextlib
    es = contextlib.ExitStack()
    for n in names:
        C[n].h = es.enter_context(nc.semaphore(n))
    with nc.Block() as block:
        @block.sync
        def _(e):
            for f in prog["sync"]:
                f(e)

        @block.gpsimd
        def _(e):
            for f in prog["gpsimd"]:
                f(e)

        @block.tensor
        def _(e):
            for f in prog["tensor"]:
                f(e)

        @block.vector
        def _(e):
            for f in prog["vector"]:
                f(e)

        @block.scalar
        def _(e):
            for f in prog["scalar"]:
                f(e)
    es.close()
    n_inst = {k: len(v) for k, v in prog.items()}
    print("instructions per engine:", n_inst, "total:", sum(n_inst.values()))
    return nc


def _get_nc():
    global _NC_CACHE
    if _NC_CACHE is None:
        _NC_CACHE = build_nc()
    return _NC_CACHE


def kernel(L0_attn_hidden_states, L0_attn_k_cache, L0_attn_v_cache,
           L0_attn_Wq, L0_attn_Wo, L0_attn_block_table, L0_ffn_kernel):
    x = np.ascontiguousarray(np.asarray(L0_attn_hidden_states, dtype=np.float32)
                             .reshape(16384, D))
    wq = np.ascontiguousarray(np.asarray(L0_attn_Wq, dtype=np.float32))
    wo = np.ascontiguousarray(np.asarray(L0_attn_Wo, dtype=np.float32))
    kc = np.ascontiguousarray(np.asarray(L0_attn_k_cache, dtype=np.float32))
    vc = np.ascontiguousarray(np.asarray(L0_attn_v_cache, dtype=np.float32))
    bt = np.ascontiguousarray(np.asarray(L0_attn_block_table, dtype=np.int32))
    fk = np.ascontiguousarray(np.asarray(L0_ffn_kernel, dtype=np.int8).reshape(512, D))

    nc = _get_nc()
    in_maps = []
    for c in range(NCORES):
        cosT, sinT = _host_tables(c)
        in_maps.append({
            "x_in": np.ascontiguousarray(x[c * T:(c + 1) * T]),
            "wq_in": wq, "wo_in": wo, "kc_in": kc, "vc_in": vc,
            "bt_in": bt, "fk_in": fk,
            "cos_in": cosT, "sin_in": sinT,
        })
    res = run_bass_kernel_spmd(nc, in_maps, core_ids=list(range(NCORES)))
    out = np.concatenate([np.asarray(r["y_out"]) for r in res.results], axis=0)
    return out.reshape(4, 4096, 4096)


if __name__ == "__main__":
    build_nc()
    print("build ok")
